# version 14
# baseline (speedup 1.0000x reference)
import numpy as np
import concourse.bass as bass
import concourse.mybir as mybir
from concourse.bass_utils import run_bass_kernel_spmd

F32 = mybir.dt.float32
BF16 = mybir.dt.bfloat16
AF = mybir.ActivationFunctionType
ALU = mybir.AluOpType

D = 1024
T = 2048
NB = 8
NG = 4
TG = 512
NCORES = 8

BOUND_F = [
    [(+1, 0, 0), (-1, 1, 1), (-1, 2, 2), (-1, 3, 3)],
    [(+1, 0, 1), (+1, 1, 0), (+1, 2, 3), (-1, 3, 2)],
    [(+1, 0, 2), (+1, 2, 0), (+1, 3, 1), (-1, 1, 3)],
    [(+1, 0, 3), (+1, 1, 2), (+1, 3, 0), (-1, 2, 1)],
]
RETR_F = [
    [(+1, 0, 0), (+1, 1, 1), (+1, 2, 2), (+1, 3, 3)],
    [(+1, 1, 0), (+1, 3, 2), (-1, 0, 1), (-1, 2, 3)],
    [(+1, 2, 0), (+1, 1, 3), (-1, 0, 2), (-1, 3, 1)],
    [(+1, 3, 0), (+1, 2, 1), (-1, 0, 3), (-1, 1, 2)],
]


class Sched:
    ENGS = ("pe", "v", "s", "gp", "sync")
    SEMS = ("pe", "v", "s", "gp", "ld", "gpd", "gpd1", "sld", "sld1", "sld2", "sld3",
            "vld", "vld1", "vld2", "vld3", "lx0", "lx1", "lx2", "lx3", "st")

    def __init__(self):
        self.ops = {e: [] for e in self.ENGS}
        self.cnt = {s: 0 for s in self.SEMS}

    ENG_SEMS = ("pe", "v", "s", "gp")

    def emit(self, eng, fn, inc=None, waits=()):
        ws = tuple((s, v) for (s, v) in waits if v and v > 0)
        self.ops[eng].append((fn, inc, ws))
        if inc is None:
            return None
        amt = 1 if inc in self.ENG_SEMS else 16
        self.cnt[inc] += amt
        return self.cnt[inc]

    def replay(self, eng_name, e, sems):
        last = {}
        for fn, inc, ws in self.ops[eng_name]:
            for s, v in ws:
                if last.get(s, 0) < v:
                    e.wait_ge(sems[s], v)
                    last[s] = v
            if fn is None:
                continue
            inst = fn(e)
            if inc is not None:
                inst.then_inc(sems[inc], 1 if inc in self.ENG_SEMS else 16)


def build_nc(taps=()):
    import contextlib
    stack = contextlib.ExitStack()
    nc = bass.Bass("TRN2", target_bir_lowering=False, debug=False)

    X_in = nc.dram_tensor("X", [D, T], F32, kind="ExternalInput").ap()
    OUT = nc.dram_tensor("OUT", [D, T], F32, kind="ExternalOutput").ap()
    w_in = {}
    for nm in ("WKT", "WVT", "WOT", "WRT", "WKFT", "WVFT"):
        w_in[nm] = nc.dram_tensor(nm, [D, D], F32, kind="ExternalInput").ap()
    c_in = {}
    for nm in ("BK", "BV", "G", "LNB", "BO", "MUR", "MUK"):
        c_in[nm] = nc.dram_tensor(nm, [128, 8], F32, kind="ExternalInput").ap()
    WROW_in = nc.dram_tensor("WROW", [1, T], F32, kind="ExternalInput").ap()
    tap_out = {}
    for nm, shape, dt in taps:
        tap_out[nm] = nc.dram_tensor("tap_" + nm, shape, dt, kind="ExternalOutput").ap()

    A = lambda nm, shape, dt: nc.alloc_sbuf_tensor(nm, shape, dt).ap()
    XS = A("XS", [128, NB * T], F32)
    b1 = A("b1", [128, 4096], BF16)
    b2 = A("b2", [128, 4096], BF16)
    keysb = A("keysb", [128, 8192], BF16)
    r_sb = keysb.bitcast(F32)
    mem = A("mem", [128, 4096], F32)
    ln1f = mem.bitcast(BF16)
    retr = A("retr", [128, 4096], BF16)
    q1 = A("q1", [128, 1024], F32)
    q2 = A("q2", [128, 1024], F32)
    kr = A("kr", [128, 1024], F32)
    krb = kr.bitcast(BF16)
    bcA = A("bcA", [128, 1024], BF16)
    bcB = A("bcB", [128, 1024], BF16)
    gs = A("gs", [128, 512], F32)
    wrow = A("wrow", [1, T], BF16)
    rowsl = A("rowsl", [1, 2048], BF16)
    rowp = A("rowp", [1, 4096], BF16)
    warena = [A(f"wa{i}", [128, 8192], BF16) for i in range(3)]
    stgB = A("stgB", [128, 1024], F32)
    consts = {nm: A("c" + nm, [128, 8], F32) for nm in
              ("BK", "BV", "G", "LNB", "BO", "MUR", "MUK")}
    cLNBb = A("cLNBb", [128, 8], BF16)
    cOMR = A("cOMR", [128, 8], F32)
    cOMK = A("cOMK", [128, 8], F32)
    carry = A("carry", [128, 8], F32)
    bpre = A("bpre", [128, 8], F32)
    cmisc = A("cmisc", [128, 4], F32)
    onesb = A("onesb", [1, 128], BF16)
    colinv = A("colinv", [128, 1], BF16)

    P = lambda nm, shape: nc.alloc_psum_tensor(nm, shape, F32).ap()
    psP = [P("psP0", [128, 512]), P("psP1", [128, 512])]
    psW = [P("psW0", [128, 512]), P("psW1", [128, 512])]
    psC = P("psC", [128, 512])
    psD = P("psD", [128, 512])
    psR0 = P("psR0", [1, 512])
    psR1 = P("psR1", [1, 512])

    sch = Sched()
    E = sch.emit

    def tap(nm, ap, waits):
        if nm in tap_out:
            E("sync", lambda e, a=ap, d=tap_out[nm]: e.dma_start(out=d, in_=a),
              inc="st", waits=waits)

    def xcol(g, j):
        return 4096 * g + 512 * j

    def gcol(j):
        return 512 * j

    def ccol(c):
        return 1024 * c

    rscr = q1[0:1, 0:512]
    r_mu0 = rowsl[0:1, 0:512]
    r_t0 = rowsl[0:1, 512:1024]
    r_mu1 = rowsl[0:1, 1024:1536]
    r_t1 = rowsl[0:1, 1536:2048]
    r_mu2 = [rowp[0:1, 1024 * g: 1024 * g + 512] for g in range(NG)]
    r_in2 = [rowp[0:1, 1024 * g + 512: 1024 * g + 1024] for g in range(NG)]
    eps24 = cmisc[0:1, 0:1]
    eps24f = cmisc[:, 0:1]
    eps5 = cmisc[0:1, 2:3]

    E("v", lambda e: e.memset(cmisc[:, 0:1], 1e-24))
    E("v", lambda e: e.memset(cmisc[:, 1:2], -0.5))
    E("v", lambda e: e.memset(cmisc[:, 2:3], 1e-5))
    E("v", lambda e: e.memset(cmisc[:, 3:4], 1.0))
    E("v", lambda e: e.memset(onesb[:], 1.0))
    E("v", lambda e: e.memset(colinv[:], 1.0 / 1024.0))
    E("v", lambda e: e.memset(cOMR[:], 1.0))
    E("v", lambda e: e.memset(cOMK[:], 1.0))
    m_v_init = E("v", lambda e: e.memset(carry[:], 0.0), inc="v")

    for j in range(NB):
        src = X_in[128 * j:128 * (j + 1), 0:TG]
        dst = XS[:, xcol(0, j):xcol(0, j) + 512]
        E("s", lambda e, s=src, d=dst: e.dma_start(out=d, in_=s), inc="lx0")

    def weight_pipeline(nm, arena_idx, dq, ce, semsN, stgs_sl, chunk=512,
                        first_cast_waits=(), dma0_waits=()):
        nslots = len(semsN)
        n = (1024 // chunk) * NB
        lds = [None] * n
        casts = [None] * n
        srcs, dsts, stgs = [], [], []
        per = 1024 // chunk
        for j in range(NB):
            for h in range(per):
                i = per * j + h
                srcs.append(w_in[nm][128 * j:128 * (j + 1),
                                     chunk * h:chunk * (h + 1)])
                stgs.append(stgs_sl[i % nslots])
                dsts.append(warena[arena_idx][:, 1024 * j + chunk * h:
                                              1024 * j + chunk * (h + 1)])
        emitted = 0
        for i in range(n):
            while emitted < min(i + nslots, n):
                k = emitted
                sem = semsN[k % nslots]
                if k >= nslots:
                    wd = [(ce, casts[k - nslots])]
                else:
                    wd = list(dma0_waits)
                lds[k] = E(dq, lambda e, s=srcs[k], d=stgs[k]:
                           e.dma_start(out=d, in_=s), inc=sem, waits=wd)
                emitted += 1
            sem = semsN[i % nslots]
            w = [(sem, sch.cnt[sem])]
            if i == 0:
                w.extend(first_cast_waits)
            if ce == "s":
                casts[i] = E("s", lambda e, d=dsts[i], s=stgs[i]:
                             e.activation(out=d, in_=s, func=AF.Identity),
                             inc="s", waits=w)
            elif ce == "v":
                casts[i] = E("v", lambda e, d=dsts[i], s=stgs[i]: e.tensor_copy(d, s),
                             inc="v", waits=w)
            else:
                casts[i] = E("gp", lambda e, d=dsts[i], s=stgs[i]: e.tensor_copy(d, s),
                             inc="gp", waits=w)
        return casts


    psP_free = [None, None]
    psR_free = [None]
    M = {}
    cd_free = [None]
    psW_free = [None, None]

    def stage_front(g, ea="gp"):
        for i in range(16):
            j, kind = divmod(i, 2)
            sl = i % 2
            xap = XS[:, xcol(g, j):xcol(g, j) + 512]
            dst = krb[:, 512 * sl:512 * (sl + 1)]
            w = []
            if i == 0:
                w.append(("lx%d" % g, 128))
                if g == 0:
                    w.append(("v", m_v_init))
                pv = M.get(("st0ring", g - 1))
                if pv:
                    w.append(("pe", pv))
            if i >= 2:
                w.append(("pe", M[("st0mm", g, i - 2)]))
            func = AF.Identity if kind == 0 else AF.Square
            mS = E("s", lambda e, d=dst, x=xap, f=func:
                   e.activation(out=d, in_=x, func=f), inc="s", waits=w)
            w2 = [("s", mS)]
            if i == 0 and psR_free[0]:
                w2.append(("s", psR_free[0]))
            tgt = psR0 if kind == 0 else psR1
            M[("st0mm", g, i)] = E("pe", lambda e, d=dst, t=tgt, st=(i < 2),
                                   sp=(i >= 14):
                                   e.matmul(t[:], colinv[:], d, start=st, stop=sp),
                                   inc="pe", waits=w2)
        M[("st0ring", g)] = M[("st0mm", g, 15)]
        me0 = E("s", lambda e: e.activation(out=r_mu0, in_=psR0[:], func=AF.Identity),
                inc="s", waits=[("pe", M[("st0ring", g)])])
        me1 = E("s", lambda e: e.activation(out=r_t0, in_=psR1[:], func=AF.Identity),
                inc="s")
        psR_free[0] = me1
        E(ea, lambda e: e.tensor_mul(gs[0:1, 0:512], r_mu0, r_mu0),
          waits=[("s", me1)])
        mg2 = E(ea, lambda e: e.tensor_sub(r_t0, r_t0, gs[0:1, 0:512]), inc=ea)
        E("s", lambda e: e.activation(out=r_t0, in_=r_t0, func=AF.Ln, bias=eps5,
                                      scale=1.0), waits=[(ea, mg2)])
        mse = E("s", lambda e: e.activation(out=r_t0, in_=r_t0, func=AF.Exp,
                                            scale=-0.5), inc="s")
        wbc = [("s", mse)]
        if cd_free[0]:
            wbc.append(cd_free[0])
        E("pe", lambda e: e.matmul(psC[:], onesb[:], r_mu0, start=True, stop=True),
          waits=wbc)
        mbc = E("pe", lambda e: e.matmul(psD[:], onesb[:], r_t0, start=True, stop=True),
                inc="pe")
        wev = [("pe", mbc)]
        pv = M.get(("a3", g - 1))
        if pv:
            wev.append(pv)
        E("s", lambda e: e.activation(out=bcA[:, 0:512], in_=psC[:], func=AF.Identity),
          waits=wev)
        mcd = E("s", lambda e: e.activation(out=bcA[:, 512:1024], in_=psD[:],
                                            func=AF.Identity), inc="s")
        cd_free[0] = ("s", mcd)
        for j in range(NB):
            xap = XS[:, xcol(g, j):xcol(g, j) + 512]
            bap = b1[:, gcol(j):gcol(j) + 512]
            w = []
            if j == 0:
                w.append(("s", mcd))
                pv = M.get(("st2ring", g - 1))
                if pv:
                    w.append(("pe", pv))
            E(ea, lambda e, x=xap: e.tensor_sub(gs[:], x, bcA[:, 0:512]), waits=w)
            E(ea, lambda e, b=bap: e.tensor_mul(b, gs[:], bcA[:, 512:1024]))
            mga = E(ea, lambda e, x=xap, b=bap: e.tensor_tensor(x, x, b, ALU.add),
                    inc=ea)
            M[("gpadd", g, j)] = (ea, mga)
        M[("a3", g)] = (ea, mga)

    def stage_proj(g, p):
        for m in range(NB):
            bank = m % 2
            for c in range(NB):
                lhs = warena[p][:, 1024 * c + 128 * m: 1024 * c + 128 * (m + 1)]
                rhs = b1[:, gcol(c):gcol(c) + 512]
                w = []
                if c == 0:
                    w.append(M[("a3", g)])
                    if g == 0 and m == 0:
                        w.append(m_wk_done if p == 0 else m_wv_done)
                    cons = psP_free[bank]
                    if cons:
                        w.append(cons if isinstance(cons, tuple) else ("s", cons))
                mm = E("pe", lambda e, l=lhs, r=rhs, st=(c == 0), sp=(c == NB - 1),
                       b=bank: e.matmul(psP[b][:], l, r, start=st, stop=sp),
                       inc="pe" if c == NB - 1 else None, waits=w)
            M[("kvmm", g, p, m)] = mm
            if p == 0:
                dst = keysb[:, 4096 * (g % 2) + gcol(m):4096 * (g % 2) + gcol(m) + 512]
                bias = consts["BK"][:, m:m + 1]
            else:
                dst = b2[:, gcol(m):gcol(m) + 512]
                bias = consts["BV"][:, m:m + 1]
            wev = [("pe", mm)]
            if m == 0:
                wev.append(("ld", m_ld_all))
                if p == 0:
                    pv = M.get(("retr", g - 2))
                    if pv:
                        wev.append(("v", pv))
                    pg = M.get(("retrgp", g - 2))
                    if pg:
                        wev.append(("gp", pg))
                else:
                    pv = M.get(("womm", g - 1))
                    if pv:
                        wev.append(("pe", pv))
            mev = E("s", lambda e, d=dst, b=bank, bi=bias:
                    e.activation(out=d, in_=psP[b][:], func=AF.Identity, bias=bi,
                                 scale=1.0), inc="s", waits=wev)
            psP_free[bank] = mev
        M[("evac", g, p)] = mev

    def stage_keynorm(g):
        ko = 4096 * (g % 2)
        wq = [("s", M[("evac", g, 0)])]
        pv = M.get(("vz", g - 1))
        if pv:
            wq.append(("v", pv))
        if g == 0:
            wq.append(m_wo_done)
        msq = E("s", lambda e, k=keysb[:, ko + ccol(0):ko + ccol(0) + 1024]:
                e.activation(out=q1[:], in_=k, func=AF.Square), waits=wq, inc="s")
        msq2 = E("s", lambda e, k=keysb[:, ko + ccol(1):ko + ccol(1) + 1024]:
                 e.activation(out=q2[:], in_=k, func=AF.Square), inc="s")
        mva = E("v", lambda e: e.tensor_add(q1[:], q1[:], q2[:]), inc="v",
                waits=[("s", msq2)])
        msq3 = E("s", lambda e, k=keysb[:, ko + ccol(2):ko + ccol(2) + 1024]:
                 e.activation(out=q2[:], in_=k, func=AF.Square), inc="s",
                 waits=[("v", mva)])
        mvb = E("v", lambda e: e.tensor_add(q1[:], q1[:], q2[:]), inc="v",
                waits=[("s", msq3)])
        msq4 = E("s", lambda e, k=keysb[:, ko + ccol(3):ko + ccol(3) + 1024]:
                 e.activation(out=q2[:], in_=k, func=AF.Square), inc="s",
                 waits=[("v", mvb)])
        mvc = E("v", lambda e: e.tensor_add(q1[:], q1[:], q2[:]), inc="v",
                waits=[("s", msq4)])
        E("s", lambda e: e.activation(out=q2[:], in_=q1[:], func=AF.Ln,
                                      bias=eps24f, scale=1.0), waits=[("v", mvc)])
        mse2 = E("s", lambda e: e.activation(out=q2[:], in_=q2[:], func=AF.Exp,
                                             scale=-0.5), inc="s")
        for c in range(4):
            kap = keysb[:, ko + ccol(c):ko + ccol(c) + 1024]
            E("v", lambda e, k=kap: e.tensor_mul(k, k, q2[:]),
              waits=[("s", mse2)] if c == 0 else [])

    def stage_bound(g):
        for c in range(4):
            w0 = []
            if c == 0:
                w0.append(("s", M[("evac", g, 1)]))
                pv = M.get(("s_sq1", g - 1))
                if pv:
                    w0.append(("s", pv))
            for ti, (sg_, a, b) in enumerate(BOUND_F[c]):
                vap = b2[:, ccol(a):ccol(a) + 1024]
                kap = keysb[:, 4096 * (g % 2) + ccol(b):4096 * (g % 2) + ccol(b) + 1024]
                if ti == 0:
                    E("v", lambda e, v=vap, k=kap: e.tensor_mul(q1[:], v, k), waits=w0)
                else:
                    E("v", lambda e, v=vap, k=kap: e.tensor_mul(q2[:], v, k))
                    op = ALU.add if sg_ > 0 else ALU.subtract
                    E("v", lambda e, o=op: e.tensor_tensor(q1[:], q1[:], q2[:], o))
            for h in range(2):
                cidx = 2 * c + h
                dstm = mem[:, ccol(c) + 512 * h: ccol(c) + 512 * (h + 1)]
                wsc = []
                if c == 0 and h == 0:
                    pg = M.get(("retrgp", g - 1))
                    if pg:
                        wsc.append(("gp", pg))
                mvs = E("v", lambda e, d=dstm, h=h, ci=cidx:
                        e.tensor_tensor_scan(out=d, data0=q1[:, 512 * h:512 * (h + 1)],
                                             data1=q1[:, 512 * h:512 * (h + 1)],
                                             initial=carry[:, ci:ci + 1],
                                             op0=ALU.add, op1=ALU.bypass), inc="v",
                        waits=wsc)
        for c in range(4):
            for h in range(2):
                cidx = 2 * c + h
                srcc = mem[:, ccol(c) + 512 * h + 511: ccol(c) + 512 * h + 512]
                mgc = E("v", lambda e, s=srcc, ci=cidx:
                        e.tensor_copy(carry[:, ci:ci + 1], s),
                        inc="v" if (c == 3 and h == 1) else None)
        M[("scan", g)] = mgc

    def stage_retr(g):
        w = []
        pv = M.get(("s_sq1", g - 1))
        if pv:
            w.append(("s", pv))
            w.append(("pe", M[("st1ring", g - 1)]))
        for c in range(2):
            for ti, (sg_, a, b) in enumerate(RETR_F[c]):
                map_ = mem[:, ccol(a):ccol(a) + 1024]
                kap = keysb[:, 4096 * (g % 2) + ccol(b):4096 * (g % 2) + ccol(b) + 1024]
                if ti == 0:
                    E("v", lambda e, m_=map_, k=kap: e.tensor_mul(q1[:], m_, k),
                      waits=w if c == 0 else [])
                elif ti < 3:
                    E("v", lambda e, m_=map_, k=kap: e.tensor_mul(q2[:], m_, k))
                    op = ALU.add if sg_ > 0 else ALU.subtract
                    E("v", lambda e, o=op: e.tensor_tensor(q1[:], q1[:], q2[:], o))
                else:
                    E("v", lambda e, m_=map_, k=kap: e.tensor_mul(q2[:], m_, k))
                    op = ALU.add if sg_ > 0 else ALU.subtract
                    dstr = retr[:, ccol(c):ccol(c) + 1024]
                    mvr = E("v", lambda e, d=dstr, o=op:
                            e.tensor_tensor(d, q1[:], q2[:], o), inc="v")
        M[("retr", g)] = mvr
        b2F = b2.bitcast(F32)
        ga, gb = b2F[:, 0:1024], b2F[:, 1024:2048]
        wg = [("v", M[("scan", g)])]
        if pv:
            wg.append(("s", pv))
        for c in (2, 3):
            for ti, (sg_, a, b) in enumerate(RETR_F[c]):
                map_ = mem[:, ccol(a):ccol(a) + 1024]
                kap = keysb[:, 4096 * (g % 2) + ccol(b):4096 * (g % 2) + ccol(b) + 1024]
                if ti == 0:
                    E("gp", lambda e, m_=map_, k=kap: e.tensor_mul(ga, m_, k),
                      waits=wg if c == 2 else [])
                elif ti < 3:
                    E("gp", lambda e, m_=map_, k=kap: e.tensor_mul(gb, m_, k))
                    op = ALU.add if sg_ > 0 else ALU.subtract
                    E("gp", lambda e, o=op: e.tensor_tensor(ga, ga, gb, o))
                else:
                    E("gp", lambda e, m_=map_, k=kap: e.tensor_mul(gb, m_, k))
                    op = ALU.add if sg_ > 0 else ALU.subtract
                    dstr = retr[:, ccol(c):ccol(c) + 1024]
                    mgr = E("gp", lambda e, d=dstr, o=op:
                            e.tensor_tensor(d, ga, gb, o), inc="gp")
        M[("retrgp", g)] = mgr

    def stage_ln1(g):
        q2b = q2.bitcast(BF16)
        for j in range(NB):
            rap = retr[:, gcol(j):gcol(j) + 512]
            sl = j % 4
            w = []
            if j == 0:
                w.append(("v", M[("retr", g)]))
            if j == 4:
                w.append(("gp", M[("retrgp", g)]))
            if j >= 4:
                w.append(("pe", M[("st1mm", g, j - 4)]))
            msb = E("s", lambda e, p=q2b[:, 512 * sl:512 * (sl + 1)], r=rap:
                    e.activation(out=p, in_=r, func=AF.Square), inc="s", waits=w)
            w2 = [("s", msb)]
            if j == 0 and psR_free[0]:
                w2.append(("s", psR_free[0]))
            wm = w2 if j == 0 else (
                [("gp", M[("retrgp", g)])] if j == 4 else [])
            E("pe", lambda e, r=rap, st=(j == 0), sp=(j == NB - 1):
              e.matmul(psR0[:], colinv[:], r, start=st, stop=sp), waits=wm)
            M[("st1mm", g, j)] = E("pe", lambda e, p=q2b[:, 512 * sl:512 * (sl + 1)],
                                   st=(j == 0), sp=(j == NB - 1):
                                   e.matmul(psR1[:], colinv[:], p, start=st, stop=sp),
                                   inc="pe", waits=[("s", msb)])
        M[("s_sq1", g)] = msb
        mpe1 = M[("st1mm", g, NB - 1)]
        M[("st1ring", g)] = mpe1
        me0 = E("s", lambda e: e.activation(out=r_mu1, in_=psR0[:], func=AF.Identity),
                inc="s", waits=[("pe", mpe1)])
        me1 = E("s", lambda e: e.activation(out=r_t1, in_=psR1[:], func=AF.Identity),
                inc="s")
        psR_free[0] = me1
        wsl_ = wrow[0:1, TG * g:TG * (g + 1)]
        E("gp", lambda e: e.tensor_mul(gs[0:1, 0:512], r_mu1, r_mu1),
          waits=[("s", me1)])
        E("gp", lambda e: e.tensor_sub(r_t1, r_t1, gs[0:1, 0:512]))
        E("gp", lambda e, w_=wsl_: e.tensor_mul(r_t1, r_t1, w_),
          waits=[("gp", m_gp_wrow)] if g == 0 else [])
        mg4 = E("gp", lambda e, w_=wsl_: e.tensor_mul(r_t1, r_t1, w_), inc="gp")
        E("s", lambda e: e.activation(out=r_t1, in_=r_t1, func=AF.Ln, bias=eps5,
                                      scale=1.0), waits=[("gp", mg4)])
        mse3 = E("s", lambda e: e.activation(out=r_t1, in_=r_t1, func=AF.Exp,
                                             scale=-0.5), inc="s")
        mg5 = E("gp", lambda e, w_=wsl_: e.tensor_mul(r_t1, r_t1, w_), inc="gp",
                waits=[("s", mse3)])
        wbc = [("gp", mg5)]
        if cd_free[0]:
            wbc.append(cd_free[0])
        E("pe", lambda e: e.matmul(psC[:], onesb[:], r_mu1, start=True, stop=True),
          waits=wbc)
        mbc = E("pe", lambda e: e.matmul(psD[:], onesb[:], r_t1, start=True, stop=True),
                inc="pe")
        wev = [("pe", mbc)]
        pv = M.get(("vz", g - 1))
        if pv:
            wev.append(("v", pv))
        E("s", lambda e: e.activation(out=bcB[:, 0:512], in_=psC[:], func=AF.Identity),
          waits=wev)
        mcd = E("s", lambda e: e.activation(out=bcB[:, 512:1024], in_=psD[:],
                                            func=AF.Identity), inc="s")
        cd_free[0] = ("s", mcd)
        M[("bc1ev", g)] = mcd
        if g == 0:
            tap("rows1", rowsl[0:1, 1024:2048], [("s", mcd)])

    def stage_z(g):
        for j in range(NB):
            rap = retr[:, gcol(j):gcol(j) + 512]
            zap = b2[:, gcol(j):gcol(j) + 512]
            E("v", lambda e, r=rap: e.tensor_sub(q1[:, 0:512], r, bcB[:, 0:512]),
              waits=[("s", M[("bc1ev", g)])] if j == 0 else [])
            mvz = E("v", lambda e, z=zap: e.tensor_mul(z, q1[:, 0:512],
                                                       bcB[:, 512:1024]), inc="v")
        M[("vz", g)] = mvz

    def stage_bpre():
        for j in range(NB):
            for h in range(2):
                for mm_ in range(4):
                    et = 4 * h + mm_
                    lhs = warena[2][:, 1024 * j + 512 * h + 128 * mm_:
                                    1024 * j + 512 * h + 128 * (mm_ + 1)]
                    w = []
                    st = (j == 0 and h == 0 and mm_ == 0)
                    sp = (j == NB - 1 and h == 1 and mm_ == 3)
                    if st:
                        w = [m_wo_done, ("v", m_v_lnbb)]
                    mpb = E("pe", lambda e, l=lhs, jj=j, et=et, st=st, sp=sp:
                            e.matmul(psW[0][:, et:et + 1], l, cLNBb[:, jj:jj + 1],
                                     start=st, stop=sp),
                            inc="pe" if sp else None, waits=w)
        mdone = E("v", lambda e: e.tensor_add(bpre[:], psW[0][:, 0:8], consts["BO"][:]),
                  inc="v", waits=[("pe", mpb), ("ld", m_ld_all)])
        mg = None
        for j in range(NB):
            ap = warena[2][:, 1024 * j:1024 * (j + 1)]
            mg = E("s", lambda e, a=ap, jj=j:
                   e.activation(out=a, in_=a, func=AF.Identity,
                                scale=consts["G"][:, jj:jj + 1]),
                   inc="s", waits=[("pe", mpb), ("ld", m_ld_all)])
        return mdone, mg

    def stage_wo_a17(g):
        for m in range(NB):
            bank = m % 2
            for c in range(NB):
                lhs = warena[2][:, 1024 * c + 128 * m: 1024 * c + 128 * (m + 1)]
                rhs = b2[:, gcol(c):gcol(c) + 512]
                w = []
                if c == 0:
                    if m == 0:
                        w.append(("v", M[("vz", g)]))
                        w.append(("s", m_s_gfold_done))
                        if g == 0:
                            w.append(("v", m_bpre_done))
                    cons = psW_free[bank]
                    if cons:
                        w.append(cons if isinstance(cons, tuple) else ("s", cons))
                mwo = E("pe", lambda e, l=lhs, r=rhs, st=(c == 0), sp=(c == NB - 1),
                        b=bank: e.matmul(psW[b][:], l, r, start=st, stop=sp),
                        inc="pe" if c == NB - 1 else None, waits=w)
            M[("womm_m", g, m)] = mwo
            xap = XS[:, xcol(g, m):xcol(g, m) + 512]
            wx = [("pe", mwo), M[("gpadd", g, m)]]
            if g == 0 and m == 0:
                wx.append(("v", m_bpre_done))
            mx2 = E("v", lambda e, x=xap, b=bank, mm_=m:
                    e.scalar_tensor_tensor(out=x, in0=psW[b][:],
                                           scalar=bpre[:, mm_:mm_ + 1], in1=x,
                                           op0=ALU.add, op1=ALU.add),
                    inc="v", waits=wx)
            M[("x2", g, m)] = mx2
            psW_free[bank] = ("v", mx2)
        M[("womm", g)] = mwo
        M[("x2", g)] = mx2
        if g == NG - 1:
            tap("x2full", XS[:], [("v", mx2)])

    def stage_ln2(g):
        for i in range(8):
            j = i
            sl = 2 + (i % 2)
            xap = XS[:, xcol(g, j):xcol(g, j) + 512]
            w = []
            if i == 0:
                w.append(("v", M[("x2", g)]))
                pv = M.get(("st2krb", g - 1))
                if pv:
                    w.append(("pe", pv))
            if i >= 2:
                w.append(("pe", M[("st2mm", g, i - 2)]))
            mgc = E("gp", lambda e, d=krb[:, 512 * sl:512 * (sl + 1)], x=xap:
                    e.tensor_copy(d, x), inc="gp", waits=w)
            wq = []
            if i == 0:
                nv = M.get(("kvmm", g + 1, 1, NB - 1))
                if nv:
                    wq.append(("pe", nv))
            mgs = E("gp", lambda e, p=b1[:, gcol(j):gcol(j) + 512], x=xap:
                    e.tensor_mul(p, x, x), inc="gp", waits=wq)
            w2 = [("gp", mgs)]
            if i == 0 and psR_free[0]:
                w2.append(("s", psR_free[0]))
            M[("st2mm", g, i)] = E("pe", lambda e, d=krb[:, 512 * sl:512 * (sl + 1)],
                                   st=(i == 0), sp=(i == 7):
                                   e.matmul(psR0[:], colinv[:], d, start=st, stop=sp),
                                   inc="pe", waits=w2)
            mpe2 = E("pe", lambda e, p=b1[:, gcol(j):gcol(j) + 512], st=(i == 0),
                     sp=(i == 7): e.matmul(psR1[:], colinv[:], p, start=st, stop=sp),
                     inc="pe" if i == 7 else None)
        M[("st2krb", g)] = M[("st2mm", g, 7)]
        M[("st2ring", g)] = mpe2
        me0 = E("s", lambda e, g=g: e.activation(out=r_mu2[g], in_=psR0[:],
                                                 func=AF.Identity),
                inc="s", waits=[("pe", mpe2)])
        me1 = E("s", lambda e, g=g: e.activation(out=r_in2[g], in_=psR1[:],
                                                 func=AF.Identity), inc="s")
        psR_free[0] = me1
        E("gp", lambda e, g=g: e.tensor_mul(gs[0:1, 0:512], r_mu2[g], r_mu2[g]),
          waits=[("s", me1)])
        mg2 = E("gp", lambda e, g=g: e.tensor_sub(r_in2[g], r_in2[g], gs[0:1, 0:512]),
                inc="gp")
        E("s", lambda e, g=g: e.activation(out=r_in2[g], in_=r_in2[g], func=AF.Ln,
                                           bias=eps5, scale=1.0), waits=[("gp", mg2)])
        mse4 = E("s", lambda e, g=g: e.activation(out=r_in2[g], in_=r_in2[g],
                                                  func=AF.Exp, scale=-0.5), inc="s")
        M[("ln2e", g)] = mse4
        if g == 0:
            tap("rowp", rowp[0:1, 0:1024], [("s", mse4)])

    stage_front(0, ea="v")
    wk_slots = [mem[:, 512 * i:512 * (i + 1)] for i in range(8)] + \
               [XS[:, 12288 + 512 * i:12288 + 512 * (i + 1)] for i in range(8)]
    wk_casts = []
    for j in range(NB):
        for h in range(2):
            k = 2 * j + h
            src = w_in["WKT"][128 * j:128 * (j + 1), 512 * h:512 * (h + 1)]
            E("sync", lambda e, s=src, d=wk_slots[k]: e.dma_start(out=d, in_=s),
              inc="sld")
    for j in range(NB):
        for h in range(2):
            k = 2 * j + h
            dst = warena[0][:, 1024 * j + 512 * h:1024 * j + 512 * (h + 1)]
            wk_casts.append(E("s", lambda e, d=dst, s=wk_slots[k]:
                              e.activation(out=d, in_=s, func=AF.Identity),
                              inc="s", waits=[("sld", 16 * (k + 1))]))
    m_wk_done = ("s", wk_casts[-1])
    m_ld_c = {}
    for nm in consts:
        m_ld_c[nm] = E("sync", lambda e, s=c_in[nm], d=consts[nm]:
                       e.dma_start(out=d[:], in_=s), inc="ld")
    m_ld_all = sch.cnt["ld"]
    retrF = retr.bitcast(F32)
    wo_casts = weight_pipeline(
        "WOT", 2, "s", "v", ("vld", "vld1", "vld2", "vld3"),
        [retrF[:, 0:1024], retrF[:, 1024:2048], q1[:], q2[:]], chunk=1024)
    m_wo_done = ("v", wo_casts[-1])
    m_v_lnbb = E("v", lambda e: e.tensor_copy(cLNBb[:], consts["LNB"][:]), inc="v",
                 waits=[("ld", m_ld_all)])
    E("v", lambda e: e.tensor_sub(cOMR[:], cOMR[:], consts["MUR"][:]))
    m_v_onem = E("v", lambda e: e.tensor_sub(cOMK[:], cOMK[:], consts["MUK"][:]),
                 inc="v")
    wv_casts = weight_pipeline("WVT", 1, "gp", "gp", ("gpd", "gpd1"),
                               [stgB[:, 0:512], stgB[:, 512:1024]])
    m_wv_done = ("gp", wv_casts[-1])
    tap("wk", warena[0][:], [m_wk_done])
    tap("wv", warena[1][:], [m_wv_done])
    m_gp_wrow = None
    for h in range(2):
        srcw = WROW_in[0:1, 1024 * h:1024 * (h + 1)]
        dstw = stgB[0:1, 0:1024]
        mld = E("gp", lambda e, s=srcw, d=dstw: e.dma_start(out=d, in_=s), inc="gpd",
                waits=[("gp", m_gp_wrow if m_gp_wrow else wv_casts[-1])])
        m_gp_wrow = E("gp", lambda e, h=h: e.tensor_copy(wrow[0:1, 1024 * h:1024 * (h + 1)],
                                                         stgB[0:1, 0:1024]),
                      inc="gp", waits=[("gpd", mld)])
    for g in range(1, NG):
        for j in range(NB):
            src = X_in[128 * j:128 * (j + 1), TG * g:TG * (g + 1)]
            dst = XS[:, xcol(g, j):xcol(g, j) + 512]
            w = [m_wk_done] if (g == 3 and j == 0) else []
            E("sync", lambda e, s=src, d=dst: e.dma_start(out=d, in_=s),
              inc="lx%d" % g, waits=w)
    stage_proj(0, 0)
    stage_proj(0, 1)
    m_bpre_done = None
    m_s_gfold_done = None
    wr_casts = None
    for g in range(NG):
        stage_keynorm(g)
        stage_bound(g)
        if g >= 1:
            stage_ln2(g - 1)
        if g == NG - 1:
            wr_casts = weight_pipeline(
                "WRT", 0, "gp", "gp", ("gpd", "gpd1"),
                [stgB[:, 0:512], stgB[:, 512:1024]],
                first_cast_waits=[("pe", M[("kvmm", NG - 1, 0, NB - 1)])])
        if g < NG - 1:
            stage_front(g + 1)
            stage_proj(g + 1, 0)
        stage_retr(g)
        stage_ln1(g)
        stage_z(g)
        if g == 0:
            m_bpre_done, m_s_gfold_done = stage_bpre()
        stage_wo_a17(g)
        if g < NG - 1:
            stage_proj(g + 1, 1)
    stage_ln2(NG - 1)

    ms = {
        "pe_kv": {(NG - 1, 15): M[("kvmm", NG - 1, 1, NB - 1)]},
        "pe_wo": {(NG - 1, NB - 1): M[("womm", NG - 1)]},
        "v_retr": {NG - 1: M[("retr", NG - 1)]},
        "s_sq1": {NG - 1: M[("s_sq1", NG - 1)]},
        "pe_st1": {(NG - 1, NB - 1): M[("st1ring", NG - 1)]},
        "v_z": {(NG - 1, NB - 1): M[("vz", NG - 1)]},
        "v_x2": {(g, j): M[("x2", g, j)] for g in range(NG) for j in range(NB)},
        "gp_add": {(NG - 1, NB - 1): M[("a3", NG - 1)]},
        "s_ln2e": {g: M[("ln2e", g)] for g in range(NG)},
        "s_ln2sq": {NG - 1: M[("s_sq1", NG - 1)]},
        "pe_st2": {(NG - 1, NB - 1): M[("st2ring", NG - 1)]},
        "gp_carry": {NG - 1: M[("scan", NG - 1)]},
    }

    fk = weight_pipeline("WKFT", 1, "gp", "gp", ("gpd", "gpd1"),
                         [stgB[:, 0:512], stgB[:, 512:1024]],
                         first_cast_waits=[("pe", ms["pe_kv"][(NG - 1, 15)])])
    fv = weight_pipeline("WVFT", 2, "sync", "s", ("sld", "sld1"),
                         [retrF[:, 0:512], retrF[:, 512:1024]],
                         first_cast_waits=[("pe", ms["pe_wo"][(NG - 1, NB - 1)])],
                         dma0_waits=[("v", ms["v_z"][(NG - 1, NB - 1)]),
                                     ("pe", ms["pe_st1"][(NG - 1, NB - 1)])])
    m_gp_fw = {0: ("gp", wr_casts[-1]), 1: ("gp", fk[-1]), 2: ("s", fv[-1])}
    tap("wr", warena[0][:], [m_gp_fw[0]])
    tap("wvf", warena[2][:], [m_gp_fw[2]])

    m_v_ln1f0 = E("v", lambda e: e.memset(ln1f[:, 0:NB * (TG + 2)], 0.0), inc="v",
                  waits=[("gp", M[("retrgp", NG - 1)])])
    m_v_f10 = {}
    m_pe_f = {}
    m_s_f = {}
    m_xr = {}
    m_dxk = {}
    f4_done = {}
    psF = [psP[0], psP[1], psW[0], psW[1]]
    psP_free_f = [psP_free[0], psP_free[1],
                  ("v", ms["v_x2"][(NG - 1, 6)]), ("v", ms["v_x2"][(NG - 1, 7)])]

    def stage_fprep(g):
        w = [("s", ms["s_ln2e"][g])]
        if cd_free[0]:
            w.append(cd_free[0])
        E("pe", lambda e, g=g: e.matmul(psC[:], onesb[:], r_mu2[g], start=True,
                                        stop=True), waits=w)
        m_pe_bc2 = E("pe", lambda e, g=g: e.matmul(psD[:], onesb[:], r_in2[g],
                                                   start=True, stop=True), inc="pe")
        if g > 0:
            for j in range(NB):
                src = ln1f[:, 514 * j + 513:514 * j + 514]
                dst = ln1f[:, 514 * j + 1:514 * j + 2]
                wb = [("v", m_dxk[g - 1])] if j == 0 else []
                mgb = E("s", lambda e, s=src, d=dst:
                        e.activation(out=d, in_=s, func=AF.Identity), inc="s",
                        waits=wb)
            bnd = ("s", mgb)
        else:
            bnd = ("v", m_v_ln1f0)
        for j in range(NB):
            xap = XS[:, xcol(g, j):xcol(g, j) + 512]
            lap = ln1f[:, 514 * j + 2:514 * j + 514]
            w = [("pe", m_pe_bc2)]
            if j == 0:
                w.append(bnd)
                if g == 0:
                    w.append(("v", ms["v_x2"][(g, NB - 1)]))
            E("v", lambda e, x=xap: e.tensor_sub(q1[:, 0:512], x, psC[:]), waits=w)
            mf4 = E("v", lambda e, l=lap: e.tensor_mul(l, q1[:, 0:512], psD[:]),
                    inc="v")
        cd_free[0] = ("v", mf4)
        f4_done[g] = mf4

    def stage_xr(g, w_extra):
        mxr = None
        for j in range(NB):
            dat = ln1f[:, 514 * j + 2:514 * j + 514]
            sh = ln1f[:, 514 * j + 1:514 * j + 513]
            sc = q2[:, 512 * (j % 2):512 * (j % 2 + 1)]
            wp = []
            if j == 0:
                wp.append(("v", f4_done[g]))
                if g == 0:
                    wp.append(("pe", ms["pe_st1"][(NG - 1, NB - 1)]))
            elif j >= 2:
                wp.append(("v", m_xr[(g, j - 2)]))
            msp = E("s", lambda e, s=sh, d=sc, jj=j:
                    e.activation(out=d, in_=s, func=AF.Identity,
                                 scale=cOMR[:, jj:jj + 1]), inc="s", waits=wp)
            w = [("s", msp)]
            if j == 0:
                w.append(("ld", m_ld_all))
                w.append(("v", m_v_onem))
                w.extend(w_extra)
            mxr = E("v", lambda e, d=dat, s=sc, jj=j:
                    e.scalar_tensor_tensor(out=b2[:, gcol(jj):gcol(jj) + 512],
                                           in0=d, scalar=consts["MUR"][:, jj:jj + 1],
                                           in1=s, op0=ALU.mult, op1=ALU.add),
                    inc="v", waits=w)
            m_xr[(g, j)] = mxr
        m_xr[g] = mxr

    def stage_dxk(g, w_extra):
        for j in range(NB):
            dap = b1[:, gcol(j):gcol(j) + 512]
            dat = ln1f[:, 514 * j + 2:514 * j + 514]
            sh = ln1f[:, 514 * j + 1:514 * j + 513]
            w = []
            if j == 0:
                w.append(("ld", m_ld_all))
                w.extend(w_extra)
            E("v", lambda e, d=dap, a=dat, s=sh: e.tensor_sub(d, a, s), waits=w)
            mdk = E("v", lambda e, d=dap, s=sh, jj=j:
                    e.scalar_tensor_tensor(out=d, in0=d,
                                           scalar=consts["MUK"][:, jj:jj + 1], in1=s,
                                           op0=ALU.mult, op1=ALU.add), inc="v")
        m_dxk[g] = mdk

    stage_fprep(0)
    stage_xr(0, [("pe", ms["pe_wo"][(NG - 1, NB - 1)])])
    stage_dxk(0, [("pe", ms["pe_st2"][(NG - 1, NB - 1)]),
                  ms["gp_add"][(NG - 1, NB - 1)]])
    for g in range(NG):
        if g + 1 < NG:
            stage_fprep(g + 1)
        for m in range(NB):
            bank = m % 4
            for c in range(NB):
                lhs = warena[0][:, 1024 * c + 128 * m: 1024 * c + 128 * (m + 1)]
                rhs = b2[:, gcol(c):gcol(c) + 512]
                w = []
                if c == 0:
                    w.append(("v", m_xr[g]))
                    if g == 0 and m == 0:
                        w.append(m_gp_fw[0])
                    cons = psP_free_f[bank]
                    if cons:
                        w.append(cons if isinstance(cons, tuple) else ("s", cons))
                mmf = E("pe", lambda e, l=lhs, r=rhs, st=(c == 0), sp=(c == NB - 1),
                        b=bank: e.matmul(psF[b][:], l, r, start=st, stop=sp),
                        inc="pe" if c == NB - 1 else None, waits=w)
            dst = r_sb[:, gcol(m):gcol(m) + 512]
            w = [("pe", mmf)]
            if g == 0 and m == 0:
                w.append(("v", ms["v_retr"][NG - 1]))
                w.append(("gp", M[("retrgp", NG - 1)]))
            elif g > 0 and m == 0:
                w.append(("v", m_v_f10[g - 1]))
            msf = E("s", lambda e, d=dst, b=bank:
                    e.activation(out=d, in_=psF[b][:], func=AF.Sigmoid),
                    inc="s", waits=w)
            psP_free_f[bank] = msf
        m_pe_f[(g, "r")] = mmf
        m_s_f[(g, "sig")] = msf
        if g + 1 < NG:
            stage_xr(g + 1, [("pe", m_pe_f[(g, "r")])])
        for m in range(NB):
            bank = m % 4
            for c in range(NB):
                lhs = warena[1][:, 1024 * c + 128 * m: 1024 * c + 128 * (m + 1)]
                rhs = b1[:, gcol(c):gcol(c) + 512]
                w = []
                if c == 0:
                    w.append(("v", m_dxk[g]))
                    if g == 0 and m == 0:
                        w.append(m_gp_fw[1])
                    cons = psP_free_f[bank]
                    if cons:
                        w.append(cons if isinstance(cons, tuple) else ("s", cons))
                mmf = E("pe", lambda e, l=lhs, r=rhs, st=(c == 0), sp=(c == NB - 1),
                        b=bank: e.matmul(psF[b][:], l, r, start=st, stop=sp),
                        inc="pe" if c == NB - 1 else None, waits=w)
            sl = m % 2
            krap = kr[:, 512 * sl:512 * (sl + 1)]
            dst = retr[:, gcol(m):gcol(m) + 512]
            w = [("pe", mmf)]
            if g == 0 and m == 0:
                w.append(("s", ms["s_sq1"][NG - 1]))
                w.append(("pe", ms["pe_st1"][(NG - 1, NB - 1)]))
                w.append(("v", ms["v_z"][(NG - 1, NB - 1)]))
                w.append(m_gp_fw[2])
                w.append(("pe", M[("st0ring", NG - 1)]))
                w.append(("pe", M[("st2krb", NG - 1)]))
            elif g > 0 and m == 0:
                w.append(("pe", m_pe_f[(g - 1, "v")]))
            msr = E("s", lambda e, k=krap, b=bank:
                    e.activation(out=k, in_=psF[b][:], func=AF.Relu),
                    inc="s", waits=w)
            msk = E("s", lambda e, d=dst, k=krap:
                    e.activation(out=d, in_=k, func=AF.Square), inc="s")
            psP_free_f[bank] = msk
        m_pe_f[(g, "k")] = mmf
        m_s_f[(g, "kact")] = msk
        if g + 1 < NG:
            stage_dxk(g + 1, [("pe", m_pe_f[(g, "k")])])
        for m in range(NB):
            bank = m % 4
            for c in range(NB):
                lhs = warena[2][:, 1024 * c + 128 * m: 1024 * c + 128 * (m + 1)]
                rhs = retr[:, gcol(c):gcol(c) + 512]
                w = []
                if c == 0:
                    w.append(("s", m_s_f[(g, "kact")]))
                    if g == 0 and m == 0:
                        w.append(m_gp_fw[2])
                    cons = psP_free_f[bank]
                    if cons:
                        w.append(cons if isinstance(cons, tuple) else ("s", cons))
                mmv = E("pe", lambda e, l=lhs, r=rhs, st=(c == 0), sp=(c == NB - 1), b=bank:
                        e.matmul(psF[b][:], l, r, start=st, stop=sp),
                        inc="pe" if c == NB - 1 else None, waits=w)
            xap = XS[:, xcol(g, m):xcol(g, m) + 512]
            rap = r_sb[:, gcol(m):gcol(m) + 512]
            E("v", lambda e, b=bank, r=rap: e.tensor_mul(q1[:, 0:512], psF[b][:], r),
              waits=[("pe", mmv), ("s", m_s_f[(g, "sig")])])
            mo = E("v", lambda e, x=xap: e.tensor_add(x, x, q1[:, 0:512]), inc="v")
            psP_free_f[bank] = ("v", mo)
        m_pe_f[(g, "v")] = mmv
        m_v_f10[g] = mo
        for j in range(NB):
            src = XS[:, xcol(g, j):xcol(g, j) + 512]
            dst = OUT[128 * j:128 * (j + 1), TG * g:TG * (g + 1)]
            mst = E("sync", lambda e, s=src, d=dst: e.dma_start(out=d, in_=s),
                    inc="st", waits=[("v", mo)] if j == 0 else [])

    sch.emit("sync", None, waits=[("st", sch.cnt["st"])])

    sems = {}
    for s in Sched.SEMS:
        sems[s] = stack.enter_context(nc.semaphore("sem_" + s))
    block = stack.enter_context(nc.Block())

    @block.tensor
    def _(e):
        sch.replay("pe", e, sems)

    @block.vector
    def _(e):
        sch.replay("v", e, sems)

    @block.scalar
    def _(e):
        sch.replay("s", e, sems)

    @block.gpsimd
    def _(e):
        sch.replay("gp", e, sems)

    @block.sync
    def _(e):
        sch.replay("sync", e, sems)

    return nc


_NC = None


def _get_nc():
    global _NC
    if _NC is None:
        _NC = build_nc()
    return _NC


_PERM = None


def _perm():
    global _PERM
    if _PERM is None:
        e = np.arange(D)
        _PERM = (4 * (e % 256) + e // 256).astype(np.int64)
    return _PERM


def kernel(L0_attn_x, L0_attn_Wk, L0_attn_bk, L0_attn_Wv, L0_attn_bv,
           L0_attn_ln_g, L0_attn_ln_b, L0_attn_Wo, L0_attn_bo,
           L0_ffn_W_r, L0_ffn_W_k, L0_ffn_W_v, L0_ffn_mu_r, L0_ffn_mu_k):
    perm = _perm()
    x = np.ascontiguousarray(np.asarray(L0_attn_x, dtype=np.float32))
    x_dev = np.ascontiguousarray(np.transpose(x[:, :, perm], (0, 2, 1)))

    def wt(w):
        w = np.asarray(w, dtype=np.float32)
        return np.ascontiguousarray(w[np.ix_(perm, perm)].T)

    def vec8(v):
        v = np.asarray(v, dtype=np.float32)
        return np.ascontiguousarray(v[perm].reshape(8, 128).T)

    common = {
        "WKT": wt(L0_attn_Wk), "WVT": wt(L0_attn_Wv), "WOT": wt(L0_attn_Wo),
        "WRT": wt(L0_ffn_W_r), "WKFT": wt(L0_ffn_W_k), "WVFT": wt(L0_ffn_W_v),
        "BK": vec8(L0_attn_bk), "BV": vec8(L0_attn_bv), "G": vec8(L0_attn_ln_g),
        "LNB": vec8(L0_attn_ln_b), "BO": vec8(L0_attn_bo),
        "MUR": vec8(L0_ffn_mu_r), "MUK": vec8(L0_ffn_mu_k),
        "WROW": (1.0 / np.sqrt(np.arange(1, T + 1, dtype=np.float32)))[None, :],
    }
    in_maps = []
    for b in range(NCORES):
        m = dict(common)
        m["X"] = x_dev[b]
        in_maps.append(m)

    nc = _get_nc()
    res = run_bass_kernel_spmd(nc, in_maps, core_ids=list(range(NCORES)))

    out = np.empty((NCORES, T, D), dtype=np.float32)
    for b in range(NCORES):
        od = np.asarray(res.results[b]["OUT"], dtype=np.float32)
        out[b][:, perm] = od.T
    return out


# version 15
# speedup vs baseline: 1.2456x; 1.2456x over previous
import numpy as np
import concourse.bass as bass
import concourse.mybir as mybir
from concourse.bass_utils import run_bass_kernel_spmd

F32 = mybir.dt.float32
BF16 = mybir.dt.bfloat16
AF = mybir.ActivationFunctionType
ALU = mybir.AluOpType

D = 1024
T = 2048
NB = 8
NG = 4
TG = 512
NCORES = 8

BOUND_F = [
    [(+1, 0, 0), (-1, 1, 1), (-1, 2, 2), (-1, 3, 3)],
    [(+1, 0, 1), (+1, 1, 0), (+1, 2, 3), (-1, 3, 2)],
    [(+1, 0, 2), (+1, 2, 0), (+1, 3, 1), (-1, 1, 3)],
    [(+1, 0, 3), (+1, 1, 2), (+1, 3, 0), (-1, 2, 1)],
]
RETR_F = [
    [(+1, 0, 0), (+1, 1, 1), (+1, 2, 2), (+1, 3, 3)],
    [(+1, 1, 0), (+1, 3, 2), (-1, 0, 1), (-1, 2, 3)],
    [(+1, 2, 0), (+1, 1, 3), (-1, 0, 2), (-1, 3, 1)],
    [(+1, 3, 0), (+1, 2, 1), (-1, 0, 3), (-1, 1, 2)],
]


class Sched:
    ENGS = ("pe", "v", "s", "gp", "sync")
    SEMS = ("pe", "v", "s", "gp", "ld", "gpd", "gpd1", "sld", "sld1", "sld2", "sld3",
            "vld", "vld1", "vld2", "vld3", "lx0", "lx1", "lx2", "lx3", "st")

    def __init__(self):
        self.ops = {e: [] for e in self.ENGS}
        self.cnt = {s: 0 for s in self.SEMS}

    ENG_SEMS = ("pe", "v", "s", "gp")

    def emit(self, eng, fn, inc=None, waits=()):
        ws = tuple((s, v) for (s, v) in waits if v and v > 0)
        self.ops[eng].append((fn, inc, ws))
        if inc is None:
            return None
        amt = 1 if inc in self.ENG_SEMS else 16
        self.cnt[inc] += amt
        return self.cnt[inc]

    def replay(self, eng_name, e, sems):
        last = {}
        for fn, inc, ws in self.ops[eng_name]:
            for s, v in ws:
                if last.get(s, 0) < v:
                    e.wait_ge(sems[s], v)
                    last[s] = v
            if fn is None:
                continue
            inst = fn(e)
            if inc is not None:
                inst.then_inc(sems[inc], 1 if inc in self.ENG_SEMS else 16)


def build_nc(taps=()):
    import contextlib
    stack = contextlib.ExitStack()
    nc = bass.Bass("TRN2", target_bir_lowering=False, debug=False)

    X_in = nc.dram_tensor("X", [D, T], F32, kind="ExternalInput").ap()
    OUT = nc.dram_tensor("OUT", [D, T], F32, kind="ExternalOutput").ap()
    w_in = {}
    for nm in ("WKT", "WVT", "WOT", "WRT", "WKFT", "WVFT"):
        w_in[nm] = nc.dram_tensor(nm, [D, D], F32, kind="ExternalInput").ap()
    c_in = {}
    for nm in ("BK", "BV", "G", "LNB", "BO", "MUR", "MUK"):
        c_in[nm] = nc.dram_tensor(nm, [128, 8], F32, kind="ExternalInput").ap()
    WROW_in = nc.dram_tensor("WROW", [1, T], F32, kind="ExternalInput").ap()
    tap_out = {}
    for nm, shape, dt in taps:
        tap_out[nm] = nc.dram_tensor("tap_" + nm, shape, dt, kind="ExternalOutput").ap()

    A = lambda nm, shape, dt: nc.alloc_sbuf_tensor(nm, shape, dt).ap()
    XS = A("XS", [128, NB * T], F32)
    b1 = A("b1", [128, 4096], BF16)
    b2 = A("b2", [128, 4096], BF16)
    keysb = A("keysb", [128, 8192], BF16)
    r_sb = keysb.bitcast(F32)
    mem = A("mem", [128, 4096], F32)
    ln1f = mem.bitcast(BF16)
    retr = A("retr", [128, 4096], BF16)
    q1 = A("q1", [128, 1024], F32)
    q2 = A("q2", [128, 1024], F32)
    kr = A("kr", [128, 1024], F32)
    krb = kr.bitcast(BF16)
    bcA = A("bcA", [128, 1024], BF16)
    bcB = A("bcB", [128, 1024], BF16)
    gs = A("gs", [128, 512], F32)
    wrow = A("wrow", [1, T], BF16)
    rowsl = A("rowsl", [1, 2048], BF16)
    rowp = A("rowp", [1, 4096], BF16)
    warena = [A(f"wa{i}", [128, 8192], BF16) for i in range(3)]
    stgB = A("stgB", [128, 1024], F32)
    consts = {nm: A("c" + nm, [128, 8], F32) for nm in
              ("BK", "BV", "G", "LNB", "BO", "MUR", "MUK")}
    cLNBb = A("cLNBb", [128, 8], BF16)
    cOMR = A("cOMR", [128, 8], F32)
    cOMK = A("cOMK", [128, 8], F32)
    carry = A("carry", [128, 8], F32)
    bpre = A("bpre", [128, 8], F32)
    cmisc = A("cmisc", [128, 4], F32)
    onesb = A("onesb", [1, 128], BF16)
    colinv = A("colinv", [128, 1], BF16)

    P = lambda nm, shape: nc.alloc_psum_tensor(nm, shape, F32).ap()
    psP = [P("psP0", [128, 512]), P("psP1", [128, 512])]
    psW = [P("psW0", [128, 512]), P("psW1", [128, 512])]
    psC = P("psC", [128, 512])
    psD = P("psD", [128, 512])
    psR0 = P("psR0", [1, 512])
    psR1 = P("psR1", [1, 512])

    sch = Sched()
    E = sch.emit

    def tap(nm, ap, waits):
        if nm in tap_out:
            E("sync", lambda e, a=ap, d=tap_out[nm]: e.dma_start(out=d, in_=a),
              inc="st", waits=waits)

    def xcol(g, j):
        return 4096 * g + 512 * j

    def gcol(j):
        return 512 * j

    def ccol(c):
        return 1024 * c

    rscr = q1[0:1, 0:512]
    r_mu0 = rowsl[0:1, 0:512]
    r_t0 = rowsl[0:1, 512:1024]
    r_mu1 = rowsl[0:1, 1024:1536]
    r_t1 = rowsl[0:1, 1536:2048]
    r_mu2 = [rowp[0:1, 1024 * g: 1024 * g + 512] for g in range(NG)]
    r_in2 = [rowp[0:1, 1024 * g + 512: 1024 * g + 1024] for g in range(NG)]
    eps24 = cmisc[0:1, 0:1]
    eps24f = cmisc[:, 0:1]
    eps5 = cmisc[0:1, 2:3]

    E("v", lambda e: e.memset(cmisc[:, 0:1], 1e-24))
    E("v", lambda e: e.memset(cmisc[:, 1:2], -0.5))
    E("v", lambda e: e.memset(cmisc[:, 2:3], 1e-5))
    E("v", lambda e: e.memset(cmisc[:, 3:4], 1.0))
    E("v", lambda e: e.memset(onesb[:], 1.0))
    E("v", lambda e: e.memset(colinv[:], 1.0 / 1024.0))
    E("v", lambda e: e.memset(cOMR[:], 1.0))
    E("v", lambda e: e.memset(cOMK[:], 1.0))
    m_v_init = E("v", lambda e: e.memset(carry[:], 0.0), inc="v")

    for j in range(NB):
        src = X_in[128 * j:128 * (j + 1), 0:TG]
        dst = XS[:, xcol(0, j):xcol(0, j) + 512]
        E("s", lambda e, s=src, d=dst: e.dma_start(out=d, in_=s), inc="lx0")

    def weight_pipeline(nm, arena_idx, dq, ce, semsN, stgs_sl, chunk=512,
                        first_cast_waits=(), dma0_waits=()):
        nslots = len(semsN)
        n = (1024 // chunk) * NB
        lds = [None] * n
        casts = [None] * n
        srcs, dsts, stgs = [], [], []
        per = 1024 // chunk
        for j in range(NB):
            for h in range(per):
                i = per * j + h
                srcs.append(w_in[nm][128 * j:128 * (j + 1),
                                     chunk * h:chunk * (h + 1)])
                stgs.append(stgs_sl[i % nslots])
                dsts.append(warena[arena_idx][:, 1024 * j + chunk * h:
                                              1024 * j + chunk * (h + 1)])
        emitted = 0
        for i in range(n):
            while emitted < min(i + nslots, n):
                k = emitted
                sem = semsN[k % nslots]
                if k >= nslots:
                    wd = [(ce, casts[k - nslots])]
                else:
                    wd = list(dma0_waits)
                lds[k] = E(dq, lambda e, s=srcs[k], d=stgs[k]:
                           e.dma_start(out=d, in_=s), inc=sem, waits=wd)
                emitted += 1
            sem = semsN[i % nslots]
            w = [(sem, sch.cnt[sem])]
            if i == 0:
                w.extend(first_cast_waits)
            if ce == "s":
                casts[i] = E("s", lambda e, d=dsts[i], s=stgs[i]:
                             e.activation(out=d, in_=s, func=AF.Identity),
                             inc="s", waits=w)
            elif ce == "v":
                casts[i] = E("v", lambda e, d=dsts[i], s=stgs[i]: e.tensor_copy(d, s),
                             inc="v", waits=w)
            else:
                casts[i] = E("gp", lambda e, d=dsts[i], s=stgs[i]: e.tensor_copy(d, s),
                             inc="gp", waits=w)
        return casts


    psP_free = [None, None]
    psR_free = [None]
    M = {}
    cd_free = [None]
    psW_free = [None, None]

    def stage_front(g, ea="gp"):
        for i in range(16):
            j, kind = divmod(i, 2)
            sl = i % 2
            xap = XS[:, xcol(g, j):xcol(g, j) + 512]
            dst = krb[:, 512 * sl:512 * (sl + 1)]
            w = []
            if i == 0:
                w.append(("lx%d" % g, 128))
                if g == 0:
                    w.append(("v", m_v_init))
                pv = M.get(("st0ring", g - 1))
                if pv:
                    w.append(("pe", pv))
            if i >= 2:
                w.append(("pe", M[("st0mm", g, i - 2)]))
            func = AF.Identity if kind == 0 else AF.Square
            mS = E("s", lambda e, d=dst, x=xap, f=func:
                   e.activation(out=d, in_=x, func=f), inc="s", waits=w)
            w2 = [("s", mS)]
            if i == 0 and psR_free[0]:
                w2.append(("s", psR_free[0]))
            tgt = psR0 if kind == 0 else psR1
            M[("st0mm", g, i)] = E("pe", lambda e, d=dst, t=tgt, st=(i < 2),
                                   sp=(i >= 14):
                                   e.matmul(t[:], colinv[:], d, start=st, stop=sp),
                                   inc="pe", waits=w2)
        M[("st0ring", g)] = M[("st0mm", g, 15)]
        me0 = E("s", lambda e: e.activation(out=r_mu0, in_=psR0[:], func=AF.Identity),
                inc="s", waits=[("pe", M[("st0ring", g)])])
        me1 = E("s", lambda e: e.activation(out=r_t0, in_=psR1[:], func=AF.Identity),
                inc="s")
        psR_free[0] = me1
        E(ea, lambda e: e.tensor_mul(gs[0:1, 0:512], r_mu0, r_mu0),
          waits=[("s", me1)])
        mg2 = E(ea, lambda e: e.tensor_sub(r_t0, r_t0, gs[0:1, 0:512]), inc=ea)
        E("s", lambda e: e.activation(out=r_t0, in_=r_t0, func=AF.Ln, bias=eps5,
                                      scale=1.0), waits=[(ea, mg2)])
        mse = E("s", lambda e: e.activation(out=r_t0, in_=r_t0, func=AF.Exp,
                                            scale=-0.5), inc="s")
        wbc = [("s", mse)]
        if cd_free[0]:
            wbc.append(cd_free[0])
        E("pe", lambda e: e.matmul(psC[:], onesb[:], r_mu0, start=True, stop=True),
          waits=wbc)
        mbc = E("pe", lambda e: e.matmul(psD[:], onesb[:], r_t0, start=True, stop=True),
                inc="pe")
        wev = [("pe", mbc)]
        pv = M.get(("a3", g - 1))
        if pv:
            wev.append(pv)
        E("s", lambda e: e.activation(out=bcA[:, 0:512], in_=psC[:], func=AF.Identity),
          waits=wev)
        mcd = E("s", lambda e: e.activation(out=bcA[:, 512:1024], in_=psD[:],
                                            func=AF.Identity), inc="s")
        cd_free[0] = ("s", mcd)
        for j in range(NB):
            xap = XS[:, xcol(g, j):xcol(g, j) + 512]
            bap = b1[:, gcol(j):gcol(j) + 512]
            w = []
            if j == 0:
                w.append(("s", mcd))
                pv = M.get(("st2ring", g - 1))
                if pv:
                    w.append(("pe", pv))
            E(ea, lambda e, x=xap: e.tensor_sub(gs[:], x, bcA[:, 0:512]), waits=w)
            E(ea, lambda e, b=bap: e.tensor_mul(b, gs[:], bcA[:, 512:1024]))
            mga = E(ea, lambda e, x=xap, b=bap: e.tensor_tensor(x, x, b, ALU.add),
                    inc=ea)
            M[("gpadd", g, j)] = (ea, mga)
        M[("a3", g)] = (ea, mga)

    def stage_proj(g, p):
        for m in range(NB):
            bank = m % 2
            for c in range(NB):
                lhs = warena[p][:, 1024 * c + 128 * m: 1024 * c + 128 * (m + 1)]
                rhs = b1[:, gcol(c):gcol(c) + 512]
                w = []
                if c == 0:
                    w.append(M[("a3", g)])
                    if g == 0 and m == 0:
                        w.append(m_wk_done if p == 0 else m_wv_done)
                    cons = psP_free[bank]
                    if cons:
                        w.append(cons if isinstance(cons, tuple) else ("s", cons))
                mm = E("pe", lambda e, l=lhs, r=rhs, st=(c == 0), sp=(c == NB - 1),
                       b=bank: e.matmul(psP[b][:], l, r, start=st, stop=sp),
                       inc="pe" if c == NB - 1 else None, waits=w)
            M[("kvmm", g, p, m)] = mm
            if p == 0:
                dst = keysb[:, 4096 * (g % 2) + gcol(m):4096 * (g % 2) + gcol(m) + 512]
                bias = consts["BK"][:, m:m + 1]
            else:
                dst = b2[:, gcol(m):gcol(m) + 512]
                bias = consts["BV"][:, m:m + 1]
            wev = [("pe", mm)]
            if m == 0:
                wev.append(("ld", m_ld_all))
                if p == 0:
                    pv = M.get(("retr", g - 2))
                    if pv:
                        wev.append(("v", pv))
                    pg = M.get(("retrgp", g - 2))
                    if pg:
                        wev.append(("gp", pg))
                else:
                    pv = M.get(("womm", g - 1))
                    if pv:
                        wev.append(("pe", pv))
            mev = E("s", lambda e, d=dst, b=bank, bi=bias:
                    e.activation(out=d, in_=psP[b][:], func=AF.Identity, bias=bi,
                                 scale=1.0), inc="s", waits=wev)
            psP_free[bank] = mev
        M[("evac", g, p)] = mev

    def stage_keynorm(g):
        ko = 4096 * (g % 2)
        wq = [("s", M[("evac", g, 0)])]
        pv = M.get(("vz", g - 1))
        if pv:
            wq.append(("v", pv))
        if g == 0:
            wq.append(m_wo_done)
        msq = E("s", lambda e, k=keysb[:, ko + ccol(0):ko + ccol(0) + 1024]:
                e.activation(out=q1[:], in_=k, func=AF.Square), waits=wq, inc="s")
        msq2 = E("s", lambda e, k=keysb[:, ko + ccol(1):ko + ccol(1) + 1024]:
                 e.activation(out=q2[:], in_=k, func=AF.Square), inc="s")
        mva = E("v", lambda e: e.tensor_add(q1[:], q1[:], q2[:]), inc="v",
                waits=[("s", msq2)])
        msq3 = E("s", lambda e, k=keysb[:, ko + ccol(2):ko + ccol(2) + 1024]:
                 e.activation(out=q2[:], in_=k, func=AF.Square), inc="s",
                 waits=[("v", mva)])
        mvb = E("v", lambda e: e.tensor_add(q1[:], q1[:], q2[:]), inc="v",
                waits=[("s", msq3)])
        msq4 = E("s", lambda e, k=keysb[:, ko + ccol(3):ko + ccol(3) + 1024]:
                 e.activation(out=q2[:], in_=k, func=AF.Square), inc="s",
                 waits=[("v", mvb)])
        mvc = E("v", lambda e: e.tensor_add(q1[:], q1[:], q2[:]), inc="v",
                waits=[("s", msq4)])
        E("s", lambda e: e.activation(out=q2[:], in_=q1[:], func=AF.Ln,
                                      bias=eps24f, scale=1.0), waits=[("v", mvc)])
        mse2 = E("s", lambda e: e.activation(out=q2[:], in_=q2[:], func=AF.Exp,
                                             scale=-0.5), inc="s")
        for c in range(4):
            kap = keysb[:, ko + ccol(c):ko + ccol(c) + 1024]
            E("v", lambda e, k=kap: e.tensor_mul(k, k, q2[:]),
              waits=[("s", mse2)] if c == 0 else [])

    def stage_bound(g):
        for c in range(4):
            w0 = []
            if c == 0:
                w0.append(("s", M[("evac", g, 1)]))
                pv = M.get(("s_sq1", g - 1))
                if pv:
                    w0.append(("s", pv))
            for ti, (sg_, a, b) in enumerate(BOUND_F[c]):
                vap = b2[:, ccol(a):ccol(a) + 1024]
                kap = keysb[:, 4096 * (g % 2) + ccol(b):4096 * (g % 2) + ccol(b) + 1024]
                if ti == 0:
                    E("v", lambda e, v=vap, k=kap: e.tensor_mul(q1[:], v, k), waits=w0)
                else:
                    E("v", lambda e, v=vap, k=kap: e.tensor_mul(q2[:], v, k))
                    op = ALU.add if sg_ > 0 else ALU.subtract
                    E("v", lambda e, o=op: e.tensor_tensor(q1[:], q1[:], q2[:], o))
            for h in range(2):
                cidx = 2 * c + h
                dstm = mem[:, ccol(c) + 512 * h: ccol(c) + 512 * (h + 1)]
                wsc = []
                if c == 0 and h == 0:
                    pg = M.get(("retrgp", g - 1))
                    if pg:
                        wsc.append(("gp", pg))
                mvs = E("v", lambda e, d=dstm, h=h, ci=cidx:
                        e.tensor_tensor_scan(out=d, data0=q1[:, 512 * h:512 * (h + 1)],
                                             data1=q1[:, 512 * h:512 * (h + 1)],
                                             initial=carry[:, ci:ci + 1],
                                             op0=ALU.add, op1=ALU.bypass), inc="v",
                        waits=wsc)
        for c in range(4):
            for h in range(2):
                cidx = 2 * c + h
                srcc = mem[:, ccol(c) + 512 * h + 511: ccol(c) + 512 * h + 512]
                mgc = E("v", lambda e, s=srcc, ci=cidx:
                        e.tensor_copy(carry[:, ci:ci + 1], s),
                        inc="v" if (c == 3 and h == 1) else None)
        M[("scan", g)] = mgc

    def stage_retr(g):
        w = []
        pv = M.get(("s_sq1", g - 1))
        if pv:
            w.append(("s", pv))
            w.append(("pe", M[("st1ring", g - 1)]))
        for c in range(2):
            for ti, (sg_, a, b) in enumerate(RETR_F[c]):
                map_ = mem[:, ccol(a):ccol(a) + 1024]
                kap = keysb[:, 4096 * (g % 2) + ccol(b):4096 * (g % 2) + ccol(b) + 1024]
                if ti == 0:
                    E("v", lambda e, m_=map_, k=kap: e.tensor_mul(q1[:], m_, k),
                      waits=w if c == 0 else [])
                elif ti < 3:
                    E("v", lambda e, m_=map_, k=kap: e.tensor_mul(q2[:], m_, k))
                    op = ALU.add if sg_ > 0 else ALU.subtract
                    E("v", lambda e, o=op: e.tensor_tensor(q1[:], q1[:], q2[:], o))
                else:
                    E("v", lambda e, m_=map_, k=kap: e.tensor_mul(q2[:], m_, k))
                    op = ALU.add if sg_ > 0 else ALU.subtract
                    dstr = retr[:, ccol(c):ccol(c) + 1024]
                    mvr = E("v", lambda e, d=dstr, o=op:
                            e.tensor_tensor(d, q1[:], q2[:], o), inc="v")
        M[("retr", g)] = mvr
        b2F = b2.bitcast(F32)
        ga, gb = b2F[:, 0:1024], b2F[:, 1024:2048]
        wg = [("v", M[("scan", g)])]
        if pv:
            wg.append(("s", pv))
        for c in (2, 3):
            for ti, (sg_, a, b) in enumerate(RETR_F[c]):
                map_ = mem[:, ccol(a):ccol(a) + 1024]
                kap = keysb[:, 4096 * (g % 2) + ccol(b):4096 * (g % 2) + ccol(b) + 1024]
                if ti == 0:
                    E("gp", lambda e, m_=map_, k=kap: e.tensor_mul(ga, m_, k),
                      waits=wg if c == 2 else [])
                elif ti < 3:
                    E("gp", lambda e, m_=map_, k=kap: e.tensor_mul(gb, m_, k))
                    op = ALU.add if sg_ > 0 else ALU.subtract
                    E("gp", lambda e, o=op: e.tensor_tensor(ga, ga, gb, o))
                else:
                    E("gp", lambda e, m_=map_, k=kap: e.tensor_mul(gb, m_, k))
                    op = ALU.add if sg_ > 0 else ALU.subtract
                    dstr = retr[:, ccol(c):ccol(c) + 1024]
                    mgr = E("gp", lambda e, d=dstr, o=op:
                            e.tensor_tensor(d, ga, gb, o), inc="gp")
        M[("retrgp", g)] = mgr

    def stage_ln1(g):
        q2b = q2.bitcast(BF16)
        for j in range(NB):
            rap = retr[:, gcol(j):gcol(j) + 512]
            sl = j % 4
            w = []
            if j == 0:
                w.append(("v", M[("retr", g)]))
            if j == 4:
                w.append(("gp", M[("retrgp", g)]))
            if j >= 4:
                w.append(("pe", M[("st1mm", g, j - 4)]))
            msb = E("s", lambda e, p=q2b[:, 512 * sl:512 * (sl + 1)], r=rap:
                    e.activation(out=p, in_=r, func=AF.Square), inc="s", waits=w)
            w2 = [("s", msb)]
            if j == 0 and psR_free[0]:
                w2.append(("s", psR_free[0]))
            wm = w2 if j == 0 else (
                [("gp", M[("retrgp", g)])] if j == 4 else [])
            E("pe", lambda e, r=rap, st=(j == 0), sp=(j == NB - 1):
              e.matmul(psR0[:], colinv[:], r, start=st, stop=sp), waits=wm)
            M[("st1mm", g, j)] = E("pe", lambda e, p=q2b[:, 512 * sl:512 * (sl + 1)],
                                   st=(j == 0), sp=(j == NB - 1):
                                   e.matmul(psR1[:], colinv[:], p, start=st, stop=sp),
                                   inc="pe", waits=[("s", msb)])
        M[("s_sq1", g)] = msb
        mpe1 = M[("st1mm", g, NB - 1)]
        M[("st1ring", g)] = mpe1
        me0 = E("s", lambda e: e.activation(out=r_mu1, in_=psR0[:], func=AF.Identity),
                inc="s", waits=[("pe", mpe1)])
        me1 = E("s", lambda e: e.activation(out=r_t1, in_=psR1[:], func=AF.Identity),
                inc="s")
        psR_free[0] = me1
        wsl_ = wrow[0:1, TG * g:TG * (g + 1)]
        E("gp", lambda e: e.tensor_mul(gs[0:1, 0:512], r_mu1, r_mu1),
          waits=[("s", me1)])
        E("gp", lambda e: e.tensor_sub(r_t1, r_t1, gs[0:1, 0:512]))
        E("gp", lambda e, w_=wsl_: e.tensor_mul(r_t1, r_t1, w_),
          waits=[("gp", m_gp_wrow)] if g == 0 else [])
        mg4 = E("gp", lambda e, w_=wsl_: e.tensor_mul(r_t1, r_t1, w_), inc="gp")
        E("s", lambda e: e.activation(out=r_t1, in_=r_t1, func=AF.Ln, bias=eps5,
                                      scale=1.0), waits=[("gp", mg4)])
        mse3 = E("s", lambda e: e.activation(out=r_t1, in_=r_t1, func=AF.Exp,
                                             scale=-0.5), inc="s")
        mg5 = E("gp", lambda e, w_=wsl_: e.tensor_mul(r_t1, r_t1, w_), inc="gp",
                waits=[("s", mse3)])
        wbc = [("gp", mg5)]
        if cd_free[0]:
            wbc.append(cd_free[0])
        E("pe", lambda e: e.matmul(psC[:], onesb[:], r_mu1, start=True, stop=True),
          waits=wbc)
        mbc = E("pe", lambda e: e.matmul(psD[:], onesb[:], r_t1, start=True, stop=True),
                inc="pe")
        wev = [("pe", mbc)]
        pv = M.get(("vz", g - 1))
        if pv:
            wev.append(("v", pv))
        E("s", lambda e: e.activation(out=bcB[:, 0:512], in_=psC[:], func=AF.Identity),
          waits=wev)
        mcd = E("s", lambda e: e.activation(out=bcB[:, 512:1024], in_=psD[:],
                                            func=AF.Identity), inc="s")
        cd_free[0] = ("s", mcd)
        M[("bc1ev", g)] = mcd
        if g == 0:
            tap("rows1", rowsl[0:1, 1024:2048], [("s", mcd)])

    def stage_z(g):
        for j in range(NB):
            rap = retr[:, gcol(j):gcol(j) + 512]
            zap = b2[:, gcol(j):gcol(j) + 512]
            E("v", lambda e, r=rap: e.tensor_sub(q1[:, 0:512], r, bcB[:, 0:512]),
              waits=[("s", M[("bc1ev", g)])] if j == 0 else [])
            mvz = E("v", lambda e, z=zap: e.tensor_mul(z, q1[:, 0:512],
                                                       bcB[:, 512:1024]), inc="v")
        M[("vz", g)] = mvz

    def stage_bpre():
        for j in range(NB):
            for h in range(2):
                for mm_ in range(4):
                    et = 4 * h + mm_
                    lhs = warena[2][:, 1024 * j + 512 * h + 128 * mm_:
                                    1024 * j + 512 * h + 128 * (mm_ + 1)]
                    w = []
                    st = (j == 0 and h == 0 and mm_ == 0)
                    sp = (j == NB - 1 and h == 1 and mm_ == 3)
                    if st:
                        w = [m_wo_done, ("v", m_v_lnbb)]
                    mpb = E("pe", lambda e, l=lhs, jj=j, et=et, st=st, sp=sp:
                            e.matmul(psW[0][:, et:et + 1], l, cLNBb[:, jj:jj + 1],
                                     start=st, stop=sp),
                            inc="pe" if sp else None, waits=w)
        mdone = E("v", lambda e: e.tensor_add(bpre[:], psW[0][:, 0:8], consts["BO"][:]),
                  inc="v", waits=[("pe", mpb), ("ld", m_ld_all)])
        mg = None
        for j in range(NB):
            ap = warena[2][:, 1024 * j:1024 * (j + 1)]
            mg = E("s", lambda e, a=ap, jj=j:
                   e.activation(out=a, in_=a, func=AF.Identity,
                                scale=consts["G"][:, jj:jj + 1]),
                   inc="s", waits=[("pe", mpb), ("ld", m_ld_all)])
        return mdone, mg

    def stage_wo_a17(g):
        for m in range(NB):
            bank = m % 2
            for c in range(NB):
                lhs = warena[2][:, 1024 * c + 128 * m: 1024 * c + 128 * (m + 1)]
                rhs = b2[:, gcol(c):gcol(c) + 512]
                w = []
                if c == 0:
                    if m == 0:
                        w.append(("v", M[("vz", g)]))
                        w.append(("s", m_s_gfold_done))
                        if g == 0:
                            w.append(("v", m_bpre_done))
                    cons = psW_free[bank]
                    if cons:
                        w.append(cons if isinstance(cons, tuple) else ("s", cons))
                mwo = E("pe", lambda e, l=lhs, r=rhs, st=(c == 0), sp=(c == NB - 1),
                        b=bank: e.matmul(psW[b][:], l, r, start=st, stop=sp),
                        inc="pe" if c == NB - 1 else None, waits=w)
            M[("womm_m", g, m)] = mwo
            xap = XS[:, xcol(g, m):xcol(g, m) + 512]
            wx = [("pe", mwo), M[("gpadd", g, m)]]
            if g == 0 and m == 0:
                wx.append(("v", m_bpre_done))
            mx2 = E("v", lambda e, x=xap, b=bank, mm_=m:
                    e.scalar_tensor_tensor(out=x, in0=psW[b][:],
                                           scalar=bpre[:, mm_:mm_ + 1], in1=x,
                                           op0=ALU.add, op1=ALU.add),
                    inc="v", waits=wx)
            M[("x2", g, m)] = mx2
            psW_free[bank] = ("v", mx2)
        M[("womm", g)] = mwo
        M[("x2", g)] = mx2
        if g == NG - 1:
            tap("x2full", XS[:], [("v", mx2)])

    def stage_ln2(g):
        for i in range(8):
            j = i
            sl = 2 + (i % 2)
            xap = XS[:, xcol(g, j):xcol(g, j) + 512]
            w = []
            if i == 0:
                w.append(("v", M[("x2", g)]))
                pv = M.get(("st2krb", g - 1))
                if pv:
                    w.append(("pe", pv))
            if i >= 2:
                w.append(("pe", M[("st2mm", g, i - 2)]))
            mgc = E("s", lambda e, d=krb[:, 512 * sl:512 * (sl + 1)], x=xap:
                    e.activation(out=d, in_=x, func=AF.Identity), inc="s", waits=w)
            wq = []
            if i == 0:
                nv = M.get(("kvmm", g + 1, 1, NB - 1))
                if nv:
                    wq.append(("pe", nv))
            mgs = E("s", lambda e, p=b1[:, gcol(j):gcol(j) + 512], x=xap:
                    e.activation(out=p, in_=x, func=AF.Square), inc="s", waits=wq)
            w2 = [("s", mgs)]
            if i == 0 and psR_free[0]:
                w2.append(("s", psR_free[0]))
            M[("st2mm", g, i)] = E("pe", lambda e, d=krb[:, 512 * sl:512 * (sl + 1)],
                                   st=(i == 0), sp=(i == 7):
                                   e.matmul(psR0[:], colinv[:], d, start=st, stop=sp),
                                   inc="pe", waits=w2)
            mpe2 = E("pe", lambda e, p=b1[:, gcol(j):gcol(j) + 512], st=(i == 0),
                     sp=(i == 7): e.matmul(psR1[:], colinv[:], p, start=st, stop=sp),
                     inc="pe" if i == 7 else None)
        M[("st2krb", g)] = M[("st2mm", g, 7)]
        M[("st2ring", g)] = mpe2
        me0 = E("s", lambda e, g=g: e.activation(out=r_mu2[g], in_=psR0[:],
                                                 func=AF.Identity),
                inc="s", waits=[("pe", mpe2)])
        me1 = E("s", lambda e, g=g: e.activation(out=r_in2[g], in_=psR1[:],
                                                 func=AF.Identity), inc="s")
        psR_free[0] = me1
        E("gp", lambda e, g=g: e.tensor_mul(gs[0:1, 0:512], r_mu2[g], r_mu2[g]),
          waits=[("s", me1)])
        mg2 = E("gp", lambda e, g=g: e.tensor_sub(r_in2[g], r_in2[g], gs[0:1, 0:512]),
                inc="gp")
        E("s", lambda e, g=g: e.activation(out=r_in2[g], in_=r_in2[g], func=AF.Ln,
                                           bias=eps5, scale=1.0), waits=[("gp", mg2)])
        mse4 = E("s", lambda e, g=g: e.activation(out=r_in2[g], in_=r_in2[g],
                                                  func=AF.Exp, scale=-0.5), inc="s")
        M[("ln2e", g)] = mse4
        if g == 0:
            tap("rowp", rowp[0:1, 0:1024], [("s", mse4)])

    stage_front(0, ea="v")
    wk_slots = [mem[:, 512 * i:512 * (i + 1)] for i in range(8)] + \
               [XS[:, 12288 + 512 * i:12288 + 512 * (i + 1)] for i in range(8)]
    wk_casts = []
    for j in range(NB):
        for h in range(2):
            k = 2 * j + h
            src = w_in["WKT"][128 * j:128 * (j + 1), 512 * h:512 * (h + 1)]
            E("sync", lambda e, s=src, d=wk_slots[k]: e.dma_start(out=d, in_=s),
              inc="sld")
    for j in range(NB):
        for h in range(2):
            k = 2 * j + h
            dst = warena[0][:, 1024 * j + 512 * h:1024 * j + 512 * (h + 1)]
            wk_casts.append(E("s", lambda e, d=dst, s=wk_slots[k]:
                              e.activation(out=d, in_=s, func=AF.Identity),
                              inc="s", waits=[("sld", 16 * (k + 1))]))
    m_wk_done = ("s", wk_casts[-1])
    m_ld_c = {}
    for nm in consts:
        m_ld_c[nm] = E("sync", lambda e, s=c_in[nm], d=consts[nm]:
                       e.dma_start(out=d[:], in_=s), inc="ld")
    m_ld_all = sch.cnt["ld"]
    retrF = retr.bitcast(F32)
    wo_casts = weight_pipeline(
        "WOT", 2, "s", "v", ("vld", "vld1", "vld2", "vld3"),
        [retrF[:, 0:1024], retrF[:, 1024:2048], q1[:], q2[:]], chunk=1024)
    m_wo_done = ("v", wo_casts[-1])
    m_v_lnbb = E("v", lambda e: e.tensor_copy(cLNBb[:], consts["LNB"][:]), inc="v",
                 waits=[("ld", m_ld_all)])
    E("v", lambda e: e.tensor_sub(cOMR[:], cOMR[:], consts["MUR"][:]))
    m_v_onem = E("v", lambda e: e.tensor_sub(cOMK[:], cOMK[:], consts["MUK"][:]),
                 inc="v")
    wv_casts = weight_pipeline("WVT", 1, "gp", "gp", ("gpd", "gpd1"),
                               [stgB[:, 0:512], stgB[:, 512:1024]])
    m_wv_done = ("gp", wv_casts[-1])
    tap("wk", warena[0][:], [m_wk_done])
    tap("wv", warena[1][:], [m_wv_done])
    m_gp_wrow = None
    for h in range(2):
        srcw = WROW_in[0:1, 1024 * h:1024 * (h + 1)]
        dstw = stgB[0:1, 0:1024]
        mld = E("gp", lambda e, s=srcw, d=dstw: e.dma_start(out=d, in_=s), inc="gpd",
                waits=[("gp", m_gp_wrow if m_gp_wrow else wv_casts[-1])])
        m_gp_wrow = E("gp", lambda e, h=h: e.tensor_copy(wrow[0:1, 1024 * h:1024 * (h + 1)],
                                                         stgB[0:1, 0:1024]),
                      inc="gp", waits=[("gpd", mld)])
    for g in range(1, NG):
        for j in range(NB):
            src = X_in[128 * j:128 * (j + 1), TG * g:TG * (g + 1)]
            dst = XS[:, xcol(g, j):xcol(g, j) + 512]
            w = [m_wk_done] if (g == 3 and j == 0) else []
            E("sync", lambda e, s=src, d=dst: e.dma_start(out=d, in_=s),
              inc="lx%d" % g, waits=w)
    stage_proj(0, 0)
    stage_proj(0, 1)
    m_bpre_done = None
    m_s_gfold_done = None
    wr_casts = None
    for g in range(NG):
        stage_keynorm(g)
        stage_bound(g)
        if g >= 1:
            stage_ln2(g - 1)
        if g == NG - 1:
            wr_casts = weight_pipeline(
                "WRT", 0, "gp", "gp", ("gpd", "gpd1"),
                [stgB[:, 0:512], stgB[:, 512:1024]],
                first_cast_waits=[("pe", M[("kvmm", NG - 1, 0, NB - 1)])])
        if g < NG - 1:
            stage_front(g + 1)
            stage_proj(g + 1, 0)
        stage_retr(g)
        stage_ln1(g)
        stage_z(g)
        if g == 0:
            m_bpre_done, m_s_gfold_done = stage_bpre()
        stage_wo_a17(g)
        if g < NG - 1:
            stage_proj(g + 1, 1)
    stage_ln2(NG - 1)

    ms = {
        "pe_kv": {(NG - 1, 15): M[("kvmm", NG - 1, 1, NB - 1)]},
        "pe_wo": {(NG - 1, NB - 1): M[("womm", NG - 1)]},
        "v_retr": {NG - 1: M[("retr", NG - 1)]},
        "s_sq1": {NG - 1: M[("s_sq1", NG - 1)]},
        "pe_st1": {(NG - 1, NB - 1): M[("st1ring", NG - 1)]},
        "v_z": {(NG - 1, NB - 1): M[("vz", NG - 1)]},
        "v_x2": {(g, j): M[("x2", g, j)] for g in range(NG) for j in range(NB)},
        "gp_add": {(NG - 1, NB - 1): M[("a3", NG - 1)]},
        "s_ln2e": {g: M[("ln2e", g)] for g in range(NG)},
        "s_ln2sq": {NG - 1: M[("s_sq1", NG - 1)]},
        "pe_st2": {(NG - 1, NB - 1): M[("st2ring", NG - 1)]},
        "gp_carry": {NG - 1: M[("scan", NG - 1)]},
    }

    fk = weight_pipeline("WKFT", 1, "gp", "gp", ("gpd", "gpd1"),
                         [stgB[:, 0:512], stgB[:, 512:1024]],
                         first_cast_waits=[("pe", ms["pe_kv"][(NG - 1, 15)])])
    fv = weight_pipeline("WVFT", 2, "sync", "s", ("sld", "sld1"),
                         [retrF[:, 0:512], retrF[:, 512:1024]],
                         first_cast_waits=[("pe", ms["pe_wo"][(NG - 1, NB - 1)])],
                         dma0_waits=[("v", ms["v_z"][(NG - 1, NB - 1)]),
                                     ("pe", ms["pe_st1"][(NG - 1, NB - 1)])])
    m_gp_fw = {0: ("gp", wr_casts[-1]), 1: ("gp", fk[-1]), 2: ("s", fv[-1])}
    tap("wr", warena[0][:], [m_gp_fw[0]])
    tap("wvf", warena[2][:], [m_gp_fw[2]])

    m_v_ln1f0 = E("v", lambda e: e.memset(ln1f[:, 0:NB * (TG + 2)], 0.0), inc="v",
                  waits=[("gp", M[("retrgp", NG - 1)])])
    m_v_f10 = {}
    m_pe_f = {}
    m_s_f = {}
    m_xr = {}
    m_dxk = {}
    f4_done = {}
    psF = [psP[0], psP[1], psW[0], psW[1]]
    psP_free_f = [psP_free[0], psP_free[1],
                  ("v", ms["v_x2"][(NG - 1, 6)]), ("v", ms["v_x2"][(NG - 1, 7)])]

    def stage_fprep(g):
        w = [("s", ms["s_ln2e"][g])]
        if cd_free[0]:
            w.append(cd_free[0])
        E("pe", lambda e, g=g: e.matmul(psC[:], onesb[:], r_mu2[g], start=True,
                                        stop=True), waits=w)
        m_pe_bc2 = E("pe", lambda e, g=g: e.matmul(psD[:], onesb[:], r_in2[g],
                                                   start=True, stop=True), inc="pe")
        if g > 0:
            for j in range(NB):
                src = ln1f[:, 514 * j + 513:514 * j + 514]
                dst = ln1f[:, 514 * j + 1:514 * j + 2]
                wb = [("v", m_dxk[g - 1])] if j == 0 else []
                mgb = E("s", lambda e, s=src, d=dst:
                        e.activation(out=d, in_=s, func=AF.Identity), inc="s",
                        waits=wb)
            bnd = ("s", mgb)
        else:
            bnd = ("v", m_v_ln1f0)
        for j in range(NB):
            xap = XS[:, xcol(g, j):xcol(g, j) + 512]
            lap = ln1f[:, 514 * j + 2:514 * j + 514]
            w = [("pe", m_pe_bc2)]
            if j == 0:
                w.append(bnd)
                if g == 0:
                    w.append(("v", ms["v_x2"][(g, NB - 1)]))
            E("v", lambda e, x=xap: e.tensor_sub(q1[:, 0:512], x, psC[:]), waits=w)
            mf4 = E("v", lambda e, l=lap: e.tensor_mul(l, q1[:, 0:512], psD[:]),
                    inc="v")
        cd_free[0] = ("v", mf4)
        f4_done[g] = mf4

    def stage_xr(g, w_extra):
        mxr = None
        for j in range(NB):
            dat = ln1f[:, 514 * j + 2:514 * j + 514]
            sh = ln1f[:, 514 * j + 1:514 * j + 513]
            sc = q2[:, 512 * (j % 2):512 * (j % 2 + 1)]
            wp = []
            if j == 0:
                wp.append(("v", f4_done[g]))
                if g == 0:
                    wp.append(("pe", ms["pe_st1"][(NG - 1, NB - 1)]))
            elif j >= 2:
                wp.append(("v", m_xr[(g, j - 2)]))
            msp = E("s", lambda e, s=sh, d=sc, jj=j:
                    e.activation(out=d, in_=s, func=AF.Identity,
                                 scale=cOMR[:, jj:jj + 1]), inc="s", waits=wp)
            w = [("s", msp)]
            if j == 0:
                w.append(("ld", m_ld_all))
                w.append(("v", m_v_onem))
                w.extend(w_extra)
            mxr = E("v", lambda e, d=dat, s=sc, jj=j:
                    e.scalar_tensor_tensor(out=b2[:, gcol(jj):gcol(jj) + 512],
                                           in0=d, scalar=consts["MUR"][:, jj:jj + 1],
                                           in1=s, op0=ALU.mult, op1=ALU.add),
                    inc="v", waits=w)
            m_xr[(g, j)] = mxr
        m_xr[g] = mxr

    def stage_dxk(g, w_extra):
        for j in range(NB):
            dap = b1[:, gcol(j):gcol(j) + 512]
            dat = ln1f[:, 514 * j + 2:514 * j + 514]
            sh = ln1f[:, 514 * j + 1:514 * j + 513]
            w = []
            if j == 0:
                w.append(("ld", m_ld_all))
                w.extend(w_extra)
            E("v", lambda e, d=dap, a=dat, s=sh: e.tensor_sub(d, a, s), waits=w)
            mdk = E("v", lambda e, d=dap, s=sh, jj=j:
                    e.scalar_tensor_tensor(out=d, in0=d,
                                           scalar=consts["MUK"][:, jj:jj + 1], in1=s,
                                           op0=ALU.mult, op1=ALU.add), inc="v")
        m_dxk[g] = mdk

    stage_fprep(0)
    stage_xr(0, [("pe", ms["pe_wo"][(NG - 1, NB - 1)])])
    stage_dxk(0, [("pe", ms["pe_st2"][(NG - 1, NB - 1)]),
                  ms["gp_add"][(NG - 1, NB - 1)]])
    for g in range(NG):
        if g + 1 < NG:
            stage_fprep(g + 1)
        for m in range(NB):
            bank = m % 4
            for c in range(NB):
                lhs = warena[0][:, 1024 * c + 128 * m: 1024 * c + 128 * (m + 1)]
                rhs = b2[:, gcol(c):gcol(c) + 512]
                w = []
                if c == 0:
                    w.append(("v", m_xr[g]))
                    if g == 0 and m == 0:
                        w.append(m_gp_fw[0])
                    cons = psP_free_f[bank]
                    if cons:
                        w.append(cons if isinstance(cons, tuple) else ("s", cons))
                mmf = E("pe", lambda e, l=lhs, r=rhs, st=(c == 0), sp=(c == NB - 1),
                        b=bank: e.matmul(psF[b][:], l, r, start=st, stop=sp),
                        inc="pe" if c == NB - 1 else None, waits=w)
            dst = r_sb[:, gcol(m):gcol(m) + 512]
            w = [("pe", mmf)]
            if g == 0 and m == 0:
                w.append(("v", ms["v_retr"][NG - 1]))
                w.append(("gp", M[("retrgp", NG - 1)]))
            elif g > 0 and m == 0:
                w.append(("v", m_v_f10[g - 1]))
            msf = E("s", lambda e, d=dst, b=bank:
                    e.activation(out=d, in_=psF[b][:], func=AF.Sigmoid),
                    inc="s", waits=w)
            psP_free_f[bank] = msf
        m_pe_f[(g, "r")] = mmf
        m_s_f[(g, "sig")] = msf
        if g + 1 < NG:
            stage_xr(g + 1, [("pe", m_pe_f[(g, "r")])])
        for m in range(NB):
            bank = m % 4
            for c in range(NB):
                lhs = warena[1][:, 1024 * c + 128 * m: 1024 * c + 128 * (m + 1)]
                rhs = b1[:, gcol(c):gcol(c) + 512]
                w = []
                if c == 0:
                    w.append(("v", m_dxk[g]))
                    if g == 0 and m == 0:
                        w.append(m_gp_fw[1])
                    cons = psP_free_f[bank]
                    if cons:
                        w.append(cons if isinstance(cons, tuple) else ("s", cons))
                mmf = E("pe", lambda e, l=lhs, r=rhs, st=(c == 0), sp=(c == NB - 1),
                        b=bank: e.matmul(psF[b][:], l, r, start=st, stop=sp),
                        inc="pe" if c == NB - 1 else None, waits=w)
            sl = m % 2
            krap = kr[:, 512 * sl:512 * (sl + 1)]
            dst = retr[:, gcol(m):gcol(m) + 512]
            w = [("pe", mmf)]
            if g == 0 and m == 0:
                w.append(("s", ms["s_sq1"][NG - 1]))
                w.append(("pe", ms["pe_st1"][(NG - 1, NB - 1)]))
                w.append(("v", ms["v_z"][(NG - 1, NB - 1)]))
                w.append(m_gp_fw[2])
                w.append(("pe", M[("st0ring", NG - 1)]))
                w.append(("pe", M[("st2krb", NG - 1)]))
            elif g > 0 and m == 0:
                w.append(("pe", m_pe_f[(g - 1, "v")]))
            msr = E("s", lambda e, k=krap, b=bank:
                    e.activation(out=k, in_=psF[b][:], func=AF.Relu),
                    inc="s", waits=w)
            msk = E("s", lambda e, d=dst, k=krap:
                    e.activation(out=d, in_=k, func=AF.Square), inc="s")
            psP_free_f[bank] = msk
        m_pe_f[(g, "k")] = mmf
        m_s_f[(g, "kact")] = msk
        if g + 1 < NG:
            stage_dxk(g + 1, [("pe", m_pe_f[(g, "k")])])
        for m in range(NB):
            bank = m % 4
            for c in range(NB):
                lhs = warena[2][:, 1024 * c + 128 * m: 1024 * c + 128 * (m + 1)]
                rhs = retr[:, gcol(c):gcol(c) + 512]
                w = []
                if c == 0:
                    w.append(("s", m_s_f[(g, "kact")]))
                    if g == 0 and m == 0:
                        w.append(m_gp_fw[2])
                    cons = psP_free_f[bank]
                    if cons:
                        w.append(cons if isinstance(cons, tuple) else ("s", cons))
                mmv = E("pe", lambda e, l=lhs, r=rhs, st=(c == 0), sp=(c == NB - 1), b=bank:
                        e.matmul(psF[b][:], l, r, start=st, stop=sp),
                        inc="pe" if c == NB - 1 else None, waits=w)
            xap = XS[:, xcol(g, m):xcol(g, m) + 512]
            rap = r_sb[:, gcol(m):gcol(m) + 512]
            E("v", lambda e, b=bank, r=rap: e.tensor_mul(q1[:, 0:512], psF[b][:], r),
              waits=[("pe", mmv), ("s", m_s_f[(g, "sig")])])
            mo = E("v", lambda e, x=xap: e.tensor_add(x, x, q1[:, 0:512]), inc="v")
            psP_free_f[bank] = ("v", mo)
        m_pe_f[(g, "v")] = mmv
        m_v_f10[g] = mo
        for j in range(NB):
            src = XS[:, xcol(g, j):xcol(g, j) + 512]
            dst = OUT[128 * j:128 * (j + 1), TG * g:TG * (g + 1)]
            mst = E("sync", lambda e, s=src, d=dst: e.dma_start(out=d, in_=s),
                    inc="st", waits=[("v", mo)] if j == 0 else [])

    sch.emit("sync", None, waits=[("st", sch.cnt["st"])])

    sems = {}
    for s in Sched.SEMS:
        sems[s] = stack.enter_context(nc.semaphore("sem_" + s))
    block = stack.enter_context(nc.Block())

    @block.tensor
    def _(e):
        sch.replay("pe", e, sems)

    @block.vector
    def _(e):
        sch.replay("v", e, sems)

    @block.scalar
    def _(e):
        sch.replay("s", e, sems)

    @block.gpsimd
    def _(e):
        sch.replay("gp", e, sems)

    @block.sync
    def _(e):
        sch.replay("sync", e, sems)

    return nc


_NC = None


def _get_nc():
    global _NC
    if _NC is None:
        _NC = build_nc()
    return _NC


_PERM = None


def _perm():
    global _PERM
    if _PERM is None:
        e = np.arange(D)
        _PERM = (4 * (e % 256) + e // 256).astype(np.int64)
    return _PERM


def kernel(L0_attn_x, L0_attn_Wk, L0_attn_bk, L0_attn_Wv, L0_attn_bv,
           L0_attn_ln_g, L0_attn_ln_b, L0_attn_Wo, L0_attn_bo,
           L0_ffn_W_r, L0_ffn_W_k, L0_ffn_W_v, L0_ffn_mu_r, L0_ffn_mu_k):
    perm = _perm()
    x = np.ascontiguousarray(np.asarray(L0_attn_x, dtype=np.float32))
    x_dev = np.ascontiguousarray(np.transpose(x[:, :, perm], (0, 2, 1)))

    def wt(w):
        w = np.asarray(w, dtype=np.float32)
        return np.ascontiguousarray(w[np.ix_(perm, perm)].T)

    def vec8(v):
        v = np.asarray(v, dtype=np.float32)
        return np.ascontiguousarray(v[perm].reshape(8, 128).T)

    common = {
        "WKT": wt(L0_attn_Wk), "WVT": wt(L0_attn_Wv), "WOT": wt(L0_attn_Wo),
        "WRT": wt(L0_ffn_W_r), "WKFT": wt(L0_ffn_W_k), "WVFT": wt(L0_ffn_W_v),
        "BK": vec8(L0_attn_bk), "BV": vec8(L0_attn_bv), "G": vec8(L0_attn_ln_g),
        "LNB": vec8(L0_attn_ln_b), "BO": vec8(L0_attn_bo),
        "MUR": vec8(L0_ffn_mu_r), "MUK": vec8(L0_ffn_mu_k),
        "WROW": (1.0 / np.sqrt(np.arange(1, T + 1, dtype=np.float32)))[None, :],
    }
    in_maps = []
    for b in range(NCORES):
        m = dict(common)
        m["X"] = x_dev[b]
        in_maps.append(m)

    nc = _get_nc()
    res = run_bass_kernel_spmd(nc, in_maps, core_ids=list(range(NCORES)))

    out = np.empty((NCORES, T, D), dtype=np.float32)
    for b in range(NCORES):
        od = np.asarray(res.results[b]["OUT"], dtype=np.float32)
        out[b][:, perm] = od.T
    return out
